# version 09
# baseline (speedup 1.0000x reference)
import numpy as np
from contextlib import ExitStack

import ml_dtypes
import concourse.bacc as bacc
import concourse.mybir as mybir
from concourse.bass_utils import run_bass_kernel_spmd
from concourse.library_config import mlp as _MLP_LIB

F32 = mybir.dt.float32
BF16 = mybir.dt.bfloat16
I16 = mybir.dt.int16
AF = mybir.ActivationFunctionType
ALU = mybir.AluOpType
AX = mybir.AxisListType

NCORES = 8
S = 2048
D = 1024
H = 16
HD = 64
NT = S // 128
NE = 16
EPC = 2
CAP = 1024
FF = 4096
NKC = D // 128
SHIFT = 18.0
NEG = -30000.0
TBL_ROWS = EPC * (CAP + 1) + 2 + 16384
GROWS = 16384 + 1

_TAPS = set()
_STOP_AFTER = None
_CACHE = {}


def _hostprep(inputs):
    x = np.ascontiguousarray(np.asarray(inputs["L0_attn_x"], dtype=np.float32))
    wq = np.asarray(inputs["L0_attn_wq"], dtype=np.float32)
    wk = np.asarray(inputs["L0_attn_wk"], dtype=np.float32)
    wv = np.ascontiguousarray(np.asarray(inputs["L0_attn_wv"], dtype=np.float32))
    wo = np.ascontiguousarray(np.asarray(inputs["L0_attn_wo_w"], dtype=np.float32))
    wob = np.ascontiguousarray(np.asarray(inputs["L0_attn_wo_b"], dtype=np.float32).reshape(1, D))
    wg = np.ascontiguousarray(np.asarray(inputs["L0_ffn_Wg"], dtype=np.float32))
    w1 = np.asarray(inputs["L0_ffn_w1"], dtype=np.float32)
    w2 = np.asarray(inputs["L0_ffn_w2"], dtype=np.float32)

    perm = np.empty(D, dtype=np.int64)
    for h in range(H):
        for f in range(32):
            perm[h * 32 + f] = h * 64 + 2 * f
            perm[512 + h * 32 + f] = h * 64 + 2 * f + 1
    wqp = np.ascontiguousarray(wq[:, perm])
    wkp = np.ascontiguousarray(wk[:, perm])

    inv = (1.0 / (10000.0 ** (np.arange(0, HD, 2, dtype=np.float32)[: HD // 2] / np.float32(HD)))).astype(np.float32)
    angf = (np.arange(S, dtype=np.float32)[:, None] * inv[None, :]).astype(np.float32)
    cos = np.cos(angf).astype(np.float32)
    sin = np.sin(angf).astype(np.float32)
    csrep = np.zeros((S, D), np.float32)
    for h in range(H):
        csrep[:, h * 32: h * 32 + 32] = cos
        csrep[:, 512 + h * 32: 512 + h * 32 + 32] = sin

    idf = np.eye(128, dtype=np.float32)
    tri = np.zeros((128, 128), np.float32)
    for k in range(128):
        tri[k, :k] = NEG
    trib = np.asarray(tri.astype(ml_dtypes.bfloat16))
    onesb = np.asarray(np.ones((1, 128), ml_dtypes.bfloat16))
    idb = np.asarray(np.eye(128, dtype=np.float32).astype(ml_dtypes.bfloat16))
    onesf = np.ones((128, 128), np.float32)
    iota16r = np.tile(np.arange(16, dtype=np.float32)[None, :], (128, 1))
    iota16p = iota16r + 100.0
    iotac16 = np.arange(16, dtype=np.float32).reshape(16, 1)
    p = np.arange(128)
    c = np.arange(128)
    g = (p[:, None] % 16) * 1024 + c[None, :] * 8 + p[:, None] // 16
    gpad = np.zeros((128, 192), np.float32)
    gpad[:, :128] = g + 1.0
    gpd64 = np.zeros((512, 2048), np.float32)
    for qq in range(4):
        gpd64[qq * 128:(qq + 1) * 128] = np.repeat(gpad[:, qq * 32:(qq + 1) * 32], 64, axis=1)
    a = np.arange(16)
    bb = np.arange(1024)
    gwrap = (2052.0 + a[:, None] * 1024 + bb[None, :]).astype(np.float32)
    sels = np.zeros((16, 16, 16), np.float32)
    for ct in range(16):
        sels[ct, ct, :] = 1.0
    sels = sels.transpose(1, 0, 2).reshape(16, 256)

    common = dict(wq=wqp, wk=wkp, wv=wv, wo=wo, wob=wob, wg=wg, cs=csrep,
                  idf=idf, trib=trib, onesb=onesb, idb=idb, onesf=onesf,
                  iota16r=iota16r, iota16p=iota16p, iotac16=iotac16,
                  gpad=gpad, gpd64=gpd64, gwrap=gwrap, sels=sels)
    in_maps = []
    for b in range(NCORES):
        m = dict(common)
        m["xin"] = np.ascontiguousarray(x[b])
        m["w1a"] = np.ascontiguousarray(w1[2 * b])
        m["w1b"] = np.ascontiguousarray(w1[2 * b + 1])
        m["w2a"] = np.ascontiguousarray(w2[2 * b])
        m["w2b"] = np.ascontiguousarray(w2[2 * b + 1])
        pcv = np.zeros((128, 2), np.float32)
        pcv[:, 0] = b * S
        pcv[:, 1] = 2 * b
        m["pcv"] = pcv
        mexp = np.zeros((128, 16), np.float32)
        mexp[:, 0:8] = (np.arange(8)[None, :] < b).astype(np.float32)
        m["maskexp"] = mexp
        in_maps.append(m)
    return in_maps


class _K:
    def __init__(self):
        self.nc = bacc.Bacc("TRN2", target_bir_lowering=False, debug=False, detect_race_conditions=False)
        self.ctx = ExitStack()
        self.semc = {}
        self.tapped = []

    def sem(self, name):
        s = self.ctx.enter_context(self.nc.semaphore(name))
        self.semc[name] = 0
        return s

    def inc(self, instr, s, n=1):
        instr.then_inc(s, n)
        self.semc[s.name] += n
        return self.semc[s.name]

    def val(self, s):
        return self.semc[s.name]

    def sb(self, name, shape, dt):
        return self.ctx.enter_context(self.nc.sbuf_tensor(name, shape, dt))

    def alias(self, name, target, delta, shape, dt):
        base = self.nc.lookup_mloc(target).addr
        return self.nc.alloc_sbuf_tensor_at(name, shape, dt, offset=base + delta)

    def ps(self, name):
        return self.ctx.enter_context(self.nc.psum_tensor(name, [128, 512], F32))

    def dram(self, name, shape, dt, kind="Internal", shared=False):
        return self.nc.dram_tensor(name, list(shape), dt, kind=kind,
                                   addr_space="Shared" if shared else "Local")

    def tap(self, name, ap, wait):
        if name not in _TAPS:
            return
        t = self.nc.dram_tensor(f"tap_{name}", list(ap.shape), ap.dtype, kind="ExternalOutput")
        self.nc.scalar.wait_ge(wait[0], wait[1])
        self.inc(self.nc.scalar.dma_start(out=t.ap(), in_=ap), self.s_tap, 16)
        self.tapped.append(name)

    def overlap64(self, t):
        import dataclasses
        base = t[:, :]
        newap = [list(base.ap)[0], [1, 128], [1, 64]]
        return dataclasses.replace(base, ap=newap)

    def build(self):
        nc = self.nc
        sync, gp, ve, te, ac = nc.sync, nc.gpsimd, nc.vector, nc.tensor, nc.scalar

        xin = self.dram("xin", [S, D], F32, "ExternalInput")
        y = self.dram("y", [S, D], F32, "ExternalOutput")
        wq_d = self.dram("wq", [D, D], F32, "ExternalInput")
        wk_d = self.dram("wk", [D, D], F32, "ExternalInput")
        wv_d = self.dram("wv", [D, D], F32, "ExternalInput")
        wo_d = self.dram("wo", [D, D], F32, "ExternalInput")
        wob_d = self.dram("wob", [1, D], F32, "ExternalInput")
        wg_d = self.dram("wg", [D, NE], F32, "ExternalInput")
        w1_d = [self.dram("w1a", [D, FF], F32, "ExternalInput"),
                self.dram("w1b", [D, FF], F32, "ExternalInput")]
        w2_d = [self.dram("w2a", [FF, D], F32, "ExternalInput"),
                self.dram("w2b", [FF, D], F32, "ExternalInput")]
        cs_d = self.dram("cs", [S, D], F32, "ExternalInput")
        idf_d = self.dram("idf", [128, 128], F32, "ExternalInput")
        trib_d = self.dram("trib", [128, 128], BF16, "ExternalInput")
        onesb_d = self.dram("onesb", [1, 128], BF16, "ExternalInput")
        idb_d = self.dram("idb", [128, 128], BF16, "ExternalInput")
        onesf_d = self.dram("onesf", [128, 128], F32, "ExternalInput")
        iota16r_d = self.dram("iota16r", [128, 16], F32, "ExternalInput")
        iota16p_d = self.dram("iota16p", [128, 16], F32, "ExternalInput")
        iotac16_d = self.dram("iotac16", [16, 1], F32, "ExternalInput")
        gpad_d = self.dram("gpad", [128, 192], F32, "ExternalInput")
        gpd64_d = self.dram("gpd64", [512, 2048], F32, "ExternalInput")
        gwrap_d = self.dram("gwrap", [16, 1024], F32, "ExternalInput")
        pcv_d = self.dram("pcv", [128, 2], F32, "ExternalInput")
        maskexp_d = self.dram("maskexp", [128, 16], F32, "ExternalInput")
        sels_d = self.dram("sels", [16, 256], F32, "ExternalInput")

        xn_self = self.dram("xn_self", [S, D], BF16)
        xn_glob = self.dram("xn_glob", [GROWS, D], BF16, "Internal", shared=True)
        att_d = self.dram("att_d", [S, D], BF16)
        cnt_b = self.dram("cnt_b", [16, 1], F32)
        cnt_all = self.dram("cnt_all", [128, 1], F32, shared=True)
        b2i = self.dram("b2i", [2, 1024], I16)
        b2l = self.dram("b2l", [2, 1024], I16)
        rti = self.dram("rti", [16, 1024], I16, shared=True)
        rtl = self.dram("rtl", [16, 1024], I16, shared=True)
        ci_d = self.dram("ci_d", [16, 128], I16)
        tbl = self.dram("tbl", [TBL_ROWS, 64], F32)
        eo_self = self.dram("eo_self", [EPC * CAP, D], BF16)
        eo_glob = self.dram("eo_glob", [NCORES * EPC * CAP, D], BF16, shared=True)

        QF = self.sb("QF", [128, NKC, S], BF16)
        KF = self.sb("KF", [128, NKC, S], BF16)
        VB = self.sb("VB", [128, NT, H * 65], BF16)
        OF = self.sb("OF", [128, NKC, S], BF16)
        WB = self.sb("WB", [128, 3, NKC, D], BF16)
        QT = self.sb("QT", [128, 2, D], BF16)
        KT = self.sb("KT", [128, 2, D], BF16)
        QR = self.sb("QR", [128, 2, D], BF16)
        KR = self.sb("KR", [128, 2, D], BF16)
        T1 = self.sb("T1", [128, 512], BF16)
        T2 = self.sb("T2", [128, 512], BF16)
        T3 = self.sb("T3", [128, 512], BF16)
        T4 = self.sb("T4", [128, 512], BF16)
        PVT2 = self.sb("PVT2", [128, 16], F32)
        SM3 = self.sb("SM3", [128, 16], F32)
        EB = self.sb("EB", [128, 2, 1024], BF16)
        PKD = self.sb("PKD", [128, 2, 256], BF16)
        LNS = self.sb("LNS", [128, 8], F32)
        WGS = self.sb("WGS", [128, NKC, NE], F32)
        RBI = self.sb("RBI", [128, 16], F32)
        RBG = self.sb("RBG", [128, 16], F32)
        IDXR = self.sb("IDXR", [16, 128], F32)
        CNT = self.sb("CNT", [16, 1], F32)
        CALL16 = self.sb("CALL16", [16, 8], F32)
        PRV = self.sb("PRV", [16, 1], F32)
        PRVB = self.sb("PRVB", [128, 16], F32)
        LOCO = self.sb("LOCO", [128, 16], F32)
        PVT = self.sb("PVT", [128, 16], F32)
        LGO = self.sb("LGO", [128, 16], F32)
        GVO = self.sb("GVO", [128, 16], F32)
        CIO = self.sb("CIO", [128, 16], F32)
        SM1 = self.sb("SM1", [128, 4], F32)
        SM2 = self.sb("SM2", [128, 16], F32)
        PADI = self.sb("PADI", [128, 128], I16)
        TPI = self.sb("TPI", [128, 128], I16)
        GPD = self.sb("GPD", [128, 192], F32)
        CIW = self.sb("CIW", [128, 128], I16)
        TRD = self.sb("TRD", [16, 2, 64], F32)
        GIX = self.sb("GIX", [128, 2, 64], I16)
        RLX = self.sb("RLX", [128, 2], F32)
        NEGSH = self.sb("NEGSH", [128, 1], F32)
        IDF = self.sb("IDF", [128, 128], F32)
        TRI = self.sb("TRI", [128, 128], BF16)
        ONB = self.sb("ONB", [1, 128], BF16)
        IDB = self.sb("IDB", [128, 128], BF16)
        ONF = self.sb("ONF", [128, 128], F32)
        I16R = self.sb("I16R", [128, 16], F32)
        I16P = self.sb("I16P", [128, 16], F32)
        IC16 = self.sb("IC16", [16, 1], F32)
        MEXP = self.sb("MEXP", [128, 16], F32)
        PCV = self.sb("PCV", [128, 2], F32)

        XT = self.alias("XT", OF, 0, [128, 2, D], F32)
        XNT = self.alias("XNT", OF, 8192, [128, 2, D], F32)
        XF32 = self.alias("XF32", OF, 16384, [128, 2, D], F32)
        XFB = self.alias("XFB", OF, 24576, [128, 2, D], BF16)
        CST = self.alias("CST", OF, 28672, [128, 2, D], BF16)
        SQ = self.alias("SQ", XF32, 0, [128, 2, D], F32)
        W1T = self.alias("W1T", XF32, 0, [128, 2, NKC, 128], BF16)
        W2T = self.alias("W2T", XF32, 4096, [128, 2, D], BF16)
        EOS = self.alias("EOS", XFB, 0, [128, 2, D], BF16)
        WOBS = self.alias("WOBS", QT, 0, [1, D], BF16)
        IDWLOW = self.alias("IDWLOW", WB, 0, [16, 2, 1024], I16)
        EXI = self.alias("EXI", WB, 4096, [16, S], I16)
        EXT = self.alias("EXT", T3, 0, [128, 16, 16], I16)
        EXF = self.alias("EXF", T4, 0, [128, 256], F32)
        OHT = self.alias("OHT", WB, 16384, [16, S], F32)
        INCL = self.alias("INCL", WB, 24576, [16, S], F32)
        EXCL = self.alias("EXCL", WB, 32768, [16, S], F32)
        GWR = self.alias("GWR", WB, 40960, [16, 1024], F32)
        SELS = self.alias("SELS", WB, 45056, [16, 256], F32)
        SF1 = self.alias("SF1", QT, 0, [16, 1024], F32)
        SF2 = self.alias("SF2", KT, 0, [16, 1024], F32)
        SF3 = self.alias("SF3", QR, 0, [16, 1024], F32)
        SLW = self.alias("SLW", T1, 0, [128, 1024], I16)
        GSVA = self.alias("GSVA", QT, 0, [128, 32, 64], F32)
        GSVB = self.alias("GSVB", QR, 0, [128, 32, 64], F32)
        ZB = self.alias("ZB", EB, 0, [128, 512], F32)
        ZROW = self.alias("ZROW", EB, 2048, [1, 1024], BF16)
        ATS = self.alias("ATS", KR, 0, [128, 2, D], BF16)
        ATD = self.alias("ATD", KF, 0, [128, 2, D], BF16)

        SD = [self.ctx.enter_context(self.nc.psum_tensor(f"SD{i}", [128, 2, 512], F32)) for i in range(2)]
        SD0, SD1 = SD
        PS = [None, None, None, None] + [self.ps(f"PS{i}") for i in range(4, 8)]

        s_w = self.sem("s_w")
        s_wc = self.sem("s_wc")
        s_x = [self.sem("s_x0"), self.sem("s_x1")]
        s_cs = [self.sem("s_cs0"), self.sem("s_cs1")]
        s_v = self.sem("s_v")
        s_a = self.sem("s_a")
        s_t = self.sem("s_t")
        s_qkt = self.sem("s_qkt")
        s_qf = self.sem("s_qf")
        s_xn = self.sem("s_xn")
        s_cc = self.sem("s_cc")
        s_g = self.sem("s_g")
        s_sc = self.sem("s_sc")
        s_ot = self.sem("s_ot")
        s_att = self.sem("s_att")
        s_eo = self.sem("s_eo")
        s_d = self.sem("s_d")
        s_y = self.sem("s_y")
        s_ya = self.sem("s_ya")
        self.s_tap = self.sem("s_tap")

        gp.load_library(_MLP_LIB)
        self.inc(sync.dma_start(out=XT[:, 0, :], in_=xin.ap()[0:128, :]), s_x[0], 16)
        self.inc(gp.dma_start(out=CST[:, 0, :], in_=cs_d.ap()[0:128, :]), s_cs[0], 16)
        self.inc(gp.dma_start(out=WGS[:, :, :], in_=wg_d.ap().rearrange("(k p) e -> p k e", p=128)), s_w, 16)

        for dst, src in [(IDF, idf_d), (TRI, trib_d), (ONB, onesb_d), (IDB, idb_d), (ONF, onesf_d),
                         (I16R, iota16r_d), (I16P, iota16p_d), (IC16, iotac16_d),
                         (MEXP, maskexp_d), (PCV, pcv_d), (GPD, gpad_d)]:
            self.inc(sync.dma_start(out=dst[:, :], in_=src.ap()), s_wc, 16)
        wc_all = self.val(s_wc)

        for i, wd in enumerate([wq_d, wk_d, wv_d]):
            self.inc(gp.dma_start(out=WB[:, i, :, :], in_=wd.ap().rearrange("(k p) o -> p k o", p=128)), s_w, 16)
        w_init = self.val(s_w)

        self.inc(ve.memset(ZB[:, :], 0.0), s_v, 1)
        self.inc(ve.memset(ZROW[:, :], 0.0), s_v, 1)
        self.inc(ve.memset(NEGSH[:, :], -SHIFT), s_v, 1)
        v_z = self.val(s_v)
        gp.wait_ge(s_v, v_z)
        self.inc(gp.dma_start(out=xn_glob.ap()[GROWS - 1:GROWS, :], in_=ZROW[:, :]), s_sc, 16)
        tblf = tbl.ap().rearrange("r e -> (r e)")
        total = TBL_ROWS * 64
        assert total % 128 == 0
        cols_total = total // 128
        off = 0
        while off < cols_total:
            n = min(512, cols_total - off)
            self.inc(gp.dma_start(out=tblf[off * 128:(off + n) * 128].rearrange("(p f) -> p f", p=128),
                                  in_=ZB[:, 0:n]), s_sc, 16)
            off += n
        sc_zero = self.val(s_sc)

        v_rope = [0] * NT
        v_gate = [0] * NT
        a_qk = [0] * NT
        t_proj = [0] * NT
        v_evac = [0] * NT
        qkt_t = [0] * NT
        qft_t = [0] * NT
        a_sq = [0] * NT
        a_gate = [0] * NT
        te.wait_ge(s_wc, wc_all)
        for c in range(NT):
            pb = c % 2
            if c + 1 < NT:
                if c >= 1:
                    ac.wait_ge(s_v, v_rope[c - 1])
                self.inc(ac.dma_start(out=XT[:, (c + 1) % 2, :], in_=xin.ap()[(c + 1) * 128:(c + 2) * 128, :]), s_x[(c + 1) % 2], 16)
                if c >= 1:
                    gp.wait_ge(s_v, v_rope[c - 1])
                self.inc(gp.dma_start(out=CST[:, (c + 1) % 2, :], in_=cs_d.ap()[(c + 1) * 128:(c + 2) * 128, :]), s_cs[(c + 1) % 2], 16)
            ve.wait_ge(s_x[pb], 16 * (c // 2 + 1))
            ve.tensor_reduce(LNS[:, 0:1], XT[:, pb, :], AX.X, ALU.add)
            ve.drain()
            self.inc(ve.tensor_scalar(LNS[:, 1:2], LNS[:, 0:1], 1.0 / D, None, ALU.mult), s_v, 1)
            ac.wait_ge(s_x[pb], 16 * (c // 2 + 1))
            if c >= 2:
                ac.wait_ge(s_t, t_proj[c - 2])
            self.inc(ac.activation(SQ[:, pb, :], XT[:, pb, :], AF.Square, accum_out=LNS[:, 2:3]), s_a, 1)
            a_sq[c] = self.val(s_a)
            ve.wait_ge(s_a, a_sq[c])
            ve.tensor_scalar(LNS[:, 3:4], LNS[:, 2:3], 1.0 / D, None, ALU.mult)
            ve.tensor_scalar(LNS[:, 4:5], LNS[:, 1:2], LNS[:, 1:2], None, ALU.mult)
            ve.drain()
            ve.tensor_scalar(LNS[:, 5:6], LNS[:, 3:4], LNS[:, 4:5], 1e-5, ALU.subtract, ALU.add)
            ve.drain()
            self.inc(ve.reciprocal(LNS[:, 6:7], LNS[:, 5:6]), s_v, 1)
            ac.wait_ge(s_v, self.val(s_v))
            self.inc(ac.activation(LNS[:, 7:8], LNS[:, 6:7], AF.Sqrt), s_a, 1)
            ve.wait_ge(s_a, self.val(s_a))
            if c >= 2:
                ve.wait_ge(s_t, t_proj[c - 2])
                ve.wait_ge(s_xn, 16 * (c - 1))
            self.inc(ve.tensor_scalar(XNT[:, pb, :], XT[:, pb, :], LNS[:, 1:2], LNS[:, 7:8],
                                      ALU.subtract, ALU.mult), s_v, 1)
            v_xn = self.val(s_v)
            gp.wait_ge(s_v, v_xn)
            self.inc(gp.dma_start(out=xn_self.ap()[c * 128:(c + 1) * 128, :], in_=XNT[:, pb, :]), s_xn, 16)
            ev_list = []
            for k in range(NKC):
                te.wait_ge(s_v, v_xn)
                if k >= 2:
                    te.wait_ge(s_v, ev_list[k - 2])
                elif c >= 1:
                    te.wait_ge(s_v, v_gate[c - 1])
                    te.wait_ge(s_a, a_gate[c - 1])
                self.inc(te.transpose(SD0[:, k % 2, 0:128], XNT[:, pb, k * 128:(k + 1) * 128], IDF[:, :]), s_t, 1)
                ve.wait_ge(s_t, self.val(s_t))
                if k == 0:
                    ve.wait_ge(s_a, a_sq[c])
                ve.tensor_copy(XF32[:, pb, k * 128:(k + 1) * 128], SD0[:, k % 2, 0:128])
                self.inc(ve.tensor_copy(XFB[:, pb, k * 128:(k + 1) * 128], SD0[:, k % 2, 0:128]), s_v, 1)
                ev_list.append(self.val(s_v))
            v_xf = self.val(s_v)
            te.wait_ge(s_v, v_xf)
            te.wait_ge(s_w, w_init)
            if c >= 1:
                te.wait_ge(s_a, a_gate[c - 1])
                te.wait_ge(s_v, v_evac[c - 1])
                te.wait_ge(s_v, v_gate[c - 1])
            mm = None
            for k in range(NKC):
                st, sp = (k == 0), (k == NKC - 1)
                te.matmul(SD0[:, 0, 0:16], XF32[:, pb, k * 128:(k + 1) * 128], WGS[:, k, :], start=st, stop=sp)
                lhs = XFB[:, pb, k * 128:(k + 1) * 128]
                for half in range(2):
                    te.matmul(SD1[:, half, :], lhs, WB[:, 0, k, half * 512:(half + 1) * 512], start=st, stop=sp)
                    te.matmul(PS[4 + half][:, :], lhs, WB[:, 1, k, half * 512:(half + 1) * 512], start=st, stop=sp)
                    mm = te.matmul(PS[6 + half][:, :], lhs, WB[:, 2, k, half * 512:(half + 1) * 512], start=st, stop=sp)
            self.inc(mm, s_t, 1)
            t_proj[c] = self.val(s_t)
            ac.wait_ge(s_t, t_proj[c])
            if c >= 2:
                ac.wait_ge(s_v, v_rope[c - 2])
            ac.activation(QT[:, pb, 0:512], SD1[:, 0, :], AF.Copy)
            ac.activation(QT[:, pb, 512:1024], SD1[:, 1, :], AF.Copy)
            ac.activation(KT[:, pb, 0:512], PS[4][:, :], AF.Copy)
            self.inc(ac.activation(KT[:, pb, 512:1024], PS[5][:, :], AF.Copy), s_a, 1)
            a_qk[c] = self.val(s_a)
            ve.wait_ge(s_t, t_proj[c])
            vv = VB[:, c, :].rearrange("p (h e) -> p h e", e=65)
            ve.tensor_copy(vv[:, 0:8, 0:64], PS[6][:, :].rearrange("p (h e) -> p h e", e=64))
            ve.tensor_copy(vv[:, 8:16, 0:64], PS[7][:, :].rearrange("p (h e) -> p h e", e=64))
            self.inc(ve.memset(vv[:, :, 64:65], 1.0), s_v, 1)
            v_evac[c] = self.val(s_v)
            ve.wait_ge(s_wc, wc_all)
            self.inc(ve.tensor_reduce(SM1[:, 0:1], SD0[:, 0, 0:16], AX.X, ALU.max, negate=True), s_v, 1)
            ac.wait_ge(s_v, self.val(s_v))
            self.inc(ac.activation(SM2[:, 0:16], SD0[:, 0, 0:16], AF.Exp, bias=SM1[:, 0:1],
                                   accum_out=SM1[:, 1:2]), s_a, 1)
            a_gate[c] = self.val(s_a)
            ve.wait_ge(s_a, self.val(s_a))
            ve.reciprocal(RBG[:, c:c + 1], SM1[:, 1:2])
            ve.tensor_scalar(SM1[:, 2:3], SM1[:, 0:1], -1.0, None, ALU.mult)
            ve.drain()
            ve.tensor_scalar(SM2[:, 0:16], SD0[:, 0, 0:16], SM1[:, 2:3], None, ALU.is_ge)
            ve.drain()
            ve.scalar_tensor_tensor(SM3[:, 0:16], SM2[:, 0:16], -100.0, I16P[:, :], ALU.mult, ALU.add)
            ve.drain()
            self.inc(ve.tensor_reduce(RBI[:, c:c + 1], SM3[:, 0:16], AX.X, ALU.min), s_v, 1)
            v_gate[c] = self.val(s_v)
            ve.wait_ge(s_a, a_qk[c])
            ve.wait_ge(s_cs[pb], 16 * (c // 2 + 1))
            if c >= 2:
                ve.wait_ge(s_qkt, qkt_t[c - 2])
                ve.wait_ge(s_qf, qft_t[c - 2])
            for (TSRC, TDST) in ((QT, QR), (KT, KR)):
                dst = TDST[:, pb, :].rearrange("p (h u f) -> p h u f", u=2, f=32)
                ve.tensor_tensor(T1[:, :], TSRC[:, pb, 0:512], CST[:, pb, 0:512], ALU.mult)
                ve.tensor_tensor(T2[:, :], TSRC[:, pb, 512:1024], CST[:, pb, 512:1024], ALU.mult)
                ve.tensor_tensor(T3[:, :], TSRC[:, pb, 0:512], CST[:, pb, 512:1024], ALU.mult)
                ve.tensor_tensor(T4[:, :], TSRC[:, pb, 512:1024], CST[:, pb, 0:512], ALU.mult)
                ve.tensor_tensor(dst[:, :, 0, :], T1[:, :].rearrange("p (h f) -> p h f", f=32),
                                 T2[:, :].rearrange("p (h f) -> p h f", f=32), ALU.subtract)
                last_rope = ve.tensor_tensor(dst[:, :, 1, :], T3[:, :].rearrange("p (h f) -> p h f", f=32),
                                             T4[:, :].rearrange("p (h f) -> p h f", f=32), ALU.add)
            self.inc(last_rope, s_v, 1)
            v_rope[c] = self.val(s_v)
            sync.wait_ge(s_v, v_rope[c])
            self.inc(sync.dma_start(out=QF[:, :, c * 128:(c + 1) * 128], in_=QR[:, pb, :], transpose=True), s_qf, 16)
            self.inc(sync.dma_start(out=KF[:, :, c * 128:(c + 1) * 128], in_=KR[:, pb, :], transpose=True), s_qkt, 16)
            qkt_t[c] = self.val(s_qkt)
            qft_t[c] = self.val(s_qf)
        qkt_all = self.val(s_qkt)
        qft_all = self.val(s_qf)
        vA = self.val(s_v)
        tA = t_proj[NT - 1]

        self.tap("xnf", XF32[:, (NT - 1) % 2, :], (s_v, v_rope[NT - 1]))
        self.tap("qf", QF[:, :, :].rearrange("p a b -> p (a b)"), (s_qf, qft_all))
        self.tap("kf", KF[:, :, :].rearrange("p a b -> p (a b)"), (s_qkt, qkt_all))
        self.tap("vb", VB[:, :, :].rearrange("p a b -> p (a b)"), (s_v, vA))
        self.tap("rbi", RBI[:, :], (s_v, vA))
        self.tap("rbg", RBG[:, :], (s_v, vA))

        if _STOP_AFTER == "A":
            self._finish(y, VB, s_y)
            return

        aA = self.val(s_a)
        te.wait_ge(s_v, vA)
        te.wait_ge(s_a, aA)
        self.inc(te.transpose(SD0[0:16, 0, 0:128], RBI[:, :], IDF[:, :]), s_t, 1)
        ve.wait_ge(s_t, self.val(s_t))
        self.inc(ve.tensor_copy(IDXR[0:16, :], SD0[0:16, 0, 0:128]), s_v, 1)
        sync.wait_ge(s_t, tA)
        self.inc(sync.dma_start(out=SELS[:, :], in_=sels_d.ap()), s_wc, 16)
        wc_sels = self.val(s_wc)
        te.wait_ge(s_wc, wc_sels)
        ve.wait_ge(s_t, tA)
        v_oh2 = self.val(s_v)
        for ct in range(NT):
            te.wait_ge(s_v, v_oh2)
            self.inc(te.matmul(SD0[0:16, 1, 0:128], SELS[:, ct * 16:(ct + 1) * 16], IDXR[0:16, :],
                               start=True, stop=True), s_t, 1)
            ve.wait_ge(s_t, self.val(s_t))
            self.inc(ve.tensor_scalar(OHT[:, ct * 128:(ct + 1) * 128], SD0[0:16, 1, 0:128], IC16[:, 0:1],
                                      None, ALU.is_equal), s_v, 1)
            v_oh2 = self.val(s_v)
        v_oh_final = v_oh2

        RS = {}

        def emit_R1b():
            ve.drain()
            ve.tensor_tensor_scan(INCL[:, :], ONF[0:16, 0:1].broadcast_to([16, S]), OHT[:, :], 0.0, ALU.mult, ALU.add)
            ve.drain()
            ve.tensor_copy(CNT[:, :], INCL[:, S - 1:S])
            self.inc(ve.tensor_tensor(EXCL[:, :], INCL[:, :], OHT[:, :], ALU.subtract), s_v, 1)
            v_excl = self.val(s_v)
            RS["v_excl"] = v_excl
            ve.drain()
            self.inc(ve.tensor_copy(EXI[:, :], EXCL[:, :]), s_v, 1)
            v_exi = self.val(s_v)
            sync.wait_ge(s_v, v_exi)
            self.inc(sync.dma_start(out=EXT[:, :, :], in_=EXI[:, :], transpose=True), s_qkt, 16)
            RS["qkt_ext"] = self.val(s_qkt)
            gp.wait_ge(s_v, v_excl)
            self.inc(gp.dma_start(out=cnt_b.ap(), in_=CNT[:, :]), s_g, 16)
            gp.wait_ge(s_g, self.val(s_g))
            self.inc(gp.collective_compute("AllGather", ALU.bypass, replica_groups=[list(range(NCORES))],
                                           ins=[cnt_b.ap().opt()], outs=[cnt_all.ap().opt()]), s_cc, 1)
            RS["cc_counts"] = self.val(s_cc)

        gp.wait_ge(s_xn, 16 * NT)
        self.inc(gp.collective_compute("AllGather", ALU.bypass, replica_groups=[list(range(NCORES))],
                                       ins=[xn_self.ap().opt()], outs=[xn_glob.ap()[0:16384, :].opt()]),
                 s_cc, 1)
        cc_xn = self.val(s_cc)

        def emit_R2():
            ve.wait_ge(s_qkt, RS["qkt_ext"])
            ve.tensor_copy(EXF[:, :], EXT[:, :, :].rearrange("p a b -> p (a b)"))
            ve.drain()
            for ct in range(NT):
                ve.scalar_tensor_tensor(SM2[:, 0:16], I16R[:, :], RBI[:, ct:ct + 1], EXF[:, ct * 16:(ct + 1) * 16],
                                        ALU.is_equal, ALU.mult)
                ve.drain()
                self.inc(ve.tensor_reduce(LOCO[:, ct:ct + 1], SM2[:, 0:16], AX.X, ALU.add), s_v, 1)
            RS["v_loco"] = self.val(s_v)
            gp.wait_ge(s_cc, RS["cc_counts"])
            self.inc(gp.dma_start(out=CALL16[:, :],
                                  in_=cnt_all.ap().rearrange("(r e) o -> e (r o)", e=16),
                                  allow_slow_non_contiguous=True), s_g, 16)
            g_call = self.val(s_g)
            ve.wait_ge(s_g, g_call)
            ve.tensor_tensor(SM3[0:16, 0:8], CALL16[:, :], MEXP[0:16, 0:8], ALU.mult)
            ve.drain()
            self.inc(ve.tensor_reduce(PRV[:, :], SM3[0:16, 0:8], AX.X, ALU.add), s_v, 1)
            sync.wait_ge(s_v, self.val(s_v))
            self.inc(sync.dma_start(out=PRVB[0:1, 0:16], in_=PRV[:, :], allow_slow_non_contiguous=True), s_d, 16)
            r = 1
            while r < 128:
                self.inc(sync.dma_start(out=PRVB[r:2 * r, :], in_=PRVB[0:r, :]), s_d, 16)
                r *= 2
            d_prvb = self.val(s_d)
            ve.wait_ge(s_d, d_prvb)
            ve.memset(PVT[:, :], 0.0)
            ve.memset(PVT2[:, :], 0.0)
            for e in range(0, NE, 2):
                ve.scalar_tensor_tensor(SM2[:, 0:16], RBI[:, :], float(e),
                                        PRVB[:, e:e + 1].broadcast_to([128, 16]), ALU.is_equal, ALU.mult)
                ve.scalar_tensor_tensor(SM3[:, 0:16], RBI[:, :], float(e + 1),
                                        PRVB[:, e + 1:e + 2].broadcast_to([128, 16]), ALU.is_equal, ALU.mult)
                ve.tensor_tensor(PVT[:, :], PVT[:, :], SM2[:, 0:16], ALU.add)
                ve.tensor_tensor(PVT2[:, :], PVT2[:, :], SM3[:, 0:16], ALU.add)
            ve.drain()
            ve.tensor_tensor(PVT[:, :], PVT[:, :], PVT2[:, :], ALU.add)
            ve.wait_ge(s_v, RS["v_loco"])
            ve.drain()
            ve.tensor_tensor(LGO[:, :], LOCO[:, :], PVT[:, :], ALU.add)
            ve.drain()
            ve.tensor_scalar(SM2[:, 0:16], LGO[:, :], float(CAP), None, ALU.is_lt)
            ve.tensor_scalar(SM3[:, 0:16], LGO[:, :], float(CAP - 1), None, ALU.min)
            ve.drain()
            ve.tensor_tensor(GVO[:, :], RBG[:, :], SM2[:, 0:16], ALU.mult)
            self.inc(ve.scalar_tensor_tensor(CIO[:, :], RBI[:, :], float(CAP), SM3[:, 0:16], ALU.mult, ALU.add), s_v, 1)
            ve.memset(PADI[:, :], 0)
            ve.drain()
            ve.tensor_copy(PADI[:, 0:16], RBI[:, :])
            ve.tensor_copy(PADI[:, 16:32], LGO[:, :])
            self.inc(ve.tensor_copy(PADI[:, 32:48], CIO[:, :]), s_v, 1)
            RS["v_padi"] = self.val(s_v)
            sync.wait_ge(s_v, RS["v_padi"])
            self.inc(sync.dma_start_transpose(TPI[:, :], PADI[:, :]), s_qkt, 16)
            sync.wait_ge(s_qkt, self.val(s_qkt))
            self.inc(sync.dma_start(out=b2i.ap().rearrange("h (jo ji) -> (h jo) ji", ji=128), in_=TPI[0:16, :]), s_d, 16)
            self.inc(sync.dma_start(out=b2l.ap().rearrange("h (jo ji) -> (h jo) ji", ji=128), in_=TPI[16:32, :]), s_d, 16)
            self.inc(sync.dma_start(out=ci_d.ap(), in_=TPI[32:48, :]), s_d, 16)
            d_bnd = self.val(s_d)
            gp.wait_ge(s_d, d_bnd)
            self.inc(gp.collective_compute("AllGather", ALU.bypass, replica_groups=[list(range(NCORES))],
                                           ins=[b2i.ap().opt()], outs=[rti.ap().opt()]), s_cc, 1)
            self.inc(gp.collective_compute("AllGather", ALU.bypass, replica_groups=[list(range(NCORES))],
                                           ins=[b2l.ap().opt()], outs=[rtl.ap().opt()]), s_cc, 1)
            RS["cc_ag2"] = self.val(s_cc)

        def emit_R3():
            gp.wait_ge(s_cc, RS["cc_ag2"])
            gp.wait_ge(s_t, tA)
            self.inc(gp.dma_start(out=IDWLOW[:, 0, :], in_=rti.ap()), s_g, 16)
            self.inc(gp.dma_start(out=IDWLOW[:, 1, :], in_=rtl.ap()), s_g, 16)
            g_wr = self.val(s_g)
            self.inc(gp.dma_start(out=CIW[0:16, :],
                                  in_=ci_d.ap().rearrange("ct p -> (ct p)").rearrange("(j a) -> a j", a=16),
                                  allow_slow_non_contiguous=True), s_g, 16)
            gp.wait_ge(s_g, self.val(s_g))
            for r in range(1, 8):
                self.inc(gp.dma_start(out=CIW[r * 16:(r + 1) * 16, :], in_=CIW[0:16, :]), s_g, 16)
            RS["g_ciw"] = self.val(s_g)
            sync.wait_ge(s_t, tA)
            self.inc(sync.dma_start(out=GWR[:, :], in_=gwrap_d.ap()), s_wc, 16)
            wc_gwr = self.val(s_wc)
            ve.wait_ge(s_g, g_wr)
            ve.wait_ge(s_wc, wc_gwr)
            IDW = IDWLOW[0:16, 0, :]
            LOW = IDWLOW[0:16, 1, :]
            ve.tensor_scalar(SF1[:, :], IDW, PCV[0:16, 1:2], None, ALU.subtract)
            ve.drain()
            ve.tensor_scalar(SF2[:, :], SF1[:, :], 0.0, None, ALU.is_ge)
            ve.tensor_scalar(SF3[:, :], SF1[:, :], 1.5, None, ALU.is_le)
            ve.drain()
            ve.tensor_tensor(SF2[:, :], SF2[:, :], SF3[:, :], ALU.mult)
            ve.tensor_scalar(SF3[:, :], LOW, float(CAP), None, ALU.min)
            ve.drain()
            ve.scalar_tensor_tensor(SF1[:, :], SF1[:, :], float(CAP + 1), SF3[:, :], ALU.mult, ALU.add)
            ve.drain()
            ve.tensor_tensor(SF1[:, :], SF1[:, :], SF2[:, :], ALU.mult)
            ve.tensor_scalar(SF2[:, :], SF2[:, :], -1.0, 1.0, ALU.mult, ALU.add)
            ve.drain()
            ve.tensor_tensor(SF2[:, :], SF2[:, :], GWR[:, :], ALU.mult)
            ve.drain()
            ve.tensor_tensor(SF1[:, :], SF1[:, :], SF2[:, :], ALU.add)
            ve.drain()
            self.inc(ve.tensor_copy(SLW[0:16, :], SF1[:, :]), s_v, 1)
            v_slot = self.val(s_v)
            RS["v_slot"] = v_slot
            sync.wait_ge(s_v, v_slot)
            for r in range(1, 8):
                self.inc(sync.dma_start(out=SLW[r * 16:(r + 1) * 16, :], in_=SLW[0:16, :]), s_d, 16)
            d_slw = self.val(s_d)
            gp.wait_ge(s_d, d_slw)
            gp.wait_ge(s_sc, sc_zero)
            gp.wait_ge(s_wc, wc_all)
            scat_v = {}
            for qq in range(4):
                buf = [GSVA, GSVB][qq % 2]
                if qq >= 2:
                    gp.wait_ge(s_sc, scat_v[qq - 2])
                self.inc(gp.dma_start(out=buf[:, :, :],
                                      in_=gpd64_d.ap()[qq * 128:(qq + 1) * 128, :]
                                      .rearrange("p (c e) -> p c e", e=64)), s_g, 16)
                gp.wait_ge(s_g, self.val(s_g))
                self.inc(gp.dma_scatter_add(tbl.ap(), buf[:, :, :], SLW[:, qq * 256:(qq + 1) * 256],
                                            4096, 4096, 64), s_sc, 16)
                scat_v[qq] = self.val(s_sc)
            RS["sc_scat"] = self.val(s_sc)

        te.wait_ge(s_v, v_oh_final)
        t_av = {}
        a_exp = {}
        v_pk = {}
        v_msk = {}
        t_sc = {}
        self._first_exp = True
        for hp in range(8):
            if hp == 1:
                emit_R1b()
            elif hp == 3:
                emit_R2()
            elif hp == 5:
                emit_R3()
            for qb in range(4):
                q0 = qb * 512
                ktmax = 4 * qb + 4
                bank_first = [True, True]
                obk = [PS[4 + 2 * (qb % 2)], PS[5 + 2 * (qb % 2)]]

                def emit_scores(kt, hp=hp, qb=qb, q0=q0):
                    dbuf = kt % 2
                    sd = SD[dbuf]
                    qoff = max(q0, kt * 128)
                    w = q0 + 512 - qoff
                    te.wait_ge(s_qkt, qkt_t[4 * qb + 3])
                    te.wait_ge(s_qf, qft_t[4 * qb + 3])
                    if (hp, qb, kt - 2) in a_exp:
                        te.wait_ge(s_a, a_exp[(hp, qb, kt - 2)])
                    elif kt < 2 and (hp, qb) != (0, 0):
                        te.wait_ge(s_a, self._prev(a_exp, hp, qb, kt % 2))
                    diag = (kt * 128 >= q0) and (kt * 128 < q0 + 512)
                    te.matmul(sd[:, 0, 0:w], KF[0:64, hp, kt * 128:(kt + 1) * 128],
                              QF[0:64, hp, qoff:q0 + 512], start=True, stop=True)
                    mmb = te.matmul(sd[:, 1, 0:w], KF[64:128, hp, kt * 128:(kt + 1) * 128],
                                    QF[64:128, hp, qoff:q0 + 512], start=True, stop=True,
                                    tile_position=(64, 0))
                    self.inc(mmb, s_t, 1)
                    t_sc[(hp, qb, kt)] = self.val(s_t)
                    if diag:
                        ve.wait_ge(s_t, t_sc[(hp, qb, kt)])
                        ve.tensor_tensor(sd[:, 0, 0:128], sd[:, 0, 0:128], TRI[:, :], ALU.add)
                        self.inc(ve.tensor_tensor(sd[:, 1, 0:128], sd[:, 1, 0:128], TRI[:, :], ALU.add), s_v, 1)
                        v_msk[(hp, qb, kt)] = self.val(s_v)
                    ac.wait_ge(s_t, t_sc[(hp, qb, kt)])
                    if diag:
                        ac.wait_ge(s_v, v_msk[(hp, qb, kt)])
                    if self._first_exp:
                        ac.wait_ge(s_sc, sc_zero)
                        ac.wait_ge(s_v, v_z)
                        self._first_exp = False
                    if (hp, qb, kt - 2) in t_av:
                        ac.wait_ge(s_t, t_av[(hp, qb, kt - 2)])
                    elif kt < 2 and (hp, qb) != (0, 0):
                        ac.wait_ge(s_t, self._prev(t_av, hp, qb, kt % 2))
                    EBv = EB[:, dbuf, :].rearrange("p (u f) -> p u f", u=2)
                    self.inc(ac.activation(EBv[:, :, 0:w], sd[:, :, 0:w], AF.Exp,
                                           bias=NEGSH[:, 0:1], scale=0.125), s_a, 1)
                    a_exp[(hp, qb, kt)] = self.val(s_a)

                def emit_av(kt, hp=hp, qb=qb, q0=q0, bank_first=bank_first, obk=obk):
                    dbuf = kt % 2
                    qoff = max(q0, kt * 128)
                    te.wait_ge(s_a, a_exp[(hp, qb, kt)])
                    if kt == 0 and (hp, qb) != (0, 0):
                        te.wait_ge(s_v, self._prevpk_par(v_pk, hp, qb))
                    lastmm = None
                    for j in range(4):
                        qc = 4 * qb + j
                        if qc < kt:
                            continue
                        erel = qc * 128 - qoff
                        for h01 in range(2):
                            lastmm = te.matmul(obk[h01][:, j * 65:(j + 1) * 65],
                                               EB[:, dbuf, h01 * 512 + erel:h01 * 512 + erel + 128],
                                               VB[:, kt, (2 * hp + h01) * 65:(2 * hp + h01 + 1) * 65],
                                               start=bank_first[h01], stop=False, skip_group_check=True)
                            bank_first[h01] = False
                    self.inc(lastmm, s_t, 1)
                    t_av[(hp, qb, kt)] = self.val(s_t)

                emit_scores(0)
                for kt in range(ktmax):
                    if kt + 1 < ktmax:
                        emit_scores(kt + 1)
                    emit_av(kt)
                ve.wait_ge(s_t, t_av[(hp, qb, ktmax - 1)])
                for jj in range(2):
                    jp = jj % 2
                    prev_ot = self.val(s_ot) - 16
                    if prev_ot > 0:
                        ve.wait_ge(s_ot, prev_ot)
                    pkv = PKD[:, jp, :].rearrange("p (u f) -> p u f", u=2)
                    lpk = None
                    for j2 in range(2):
                        j = 2 * jj + j2
                        ve.reciprocal(RLX[:, 0:1], obk[0][:, j * 65 + 64:j * 65 + 65])
                        ve.reciprocal(RLX[:, 1:2], obk[1][:, j * 65 + 64:j * 65 + 65])
                        ve.tensor_scalar(pkv[:, j2, 0:64], obk[0][:, j * 65:j * 65 + 64], RLX[:, 0:1], None, ALU.mult)
                        lpk = ve.tensor_scalar(pkv[:, j2, 64:128], obk[1][:, j * 65:j * 65 + 64], RLX[:, 1:2], None, ALU.mult)
                    self.inc(lpk, s_v, 1)
                    sync.wait_ge(s_v, self.val(s_v))
                    if self.val(s_ot) == 0:
                        sync.wait_ge(s_xn, 16 * NT)
                    base = (4 * qb + 2 * jj) * 128
                    self.inc(sync.dma_start(out=OF[:, hp, base:base + 256].rearrange("p (t f) -> p t f", t=2),
                                            in_=PKD[:, jp, :], transpose=True), s_ot, 16)
                v_pk[(hp, qb)] = self.val(s_v)
        ot_all = self.val(s_ot)
        t_attn_end = self.val(s_t)
        self.tap("of", OF[:, :, :].rearrange("p a b -> p (a b)"), (s_ot, ot_all))

        gp.wait_ge(s_v, RS["v_slot"])
        self.inc(gp.dma_start(out=WB[:, 0, :, :], in_=wo_d.ap().rearrange("(k p) o -> p k o", p=128)), s_w, 16)
        gp.wait_ge(s_v, vA)
        gp.wait_ge(s_sc, RS["sc_scat"])
        self.inc(gp.dma_start(out=WOBS[:, :], in_=wob_d.ap()), s_w, 16)
        w_wo = self.val(s_w)
        v_att_evac = 0
        a_att = 0
        v_pk_last = v_pk[(7, 3)]
        ve.wait_ge(s_sc, RS["sc_scat"])
        for tc in range(NT):
            for half in range(2):
                bank = PS[6 + half]
                te.wait_ge(s_ot, ot_all)
                te.wait_ge(s_v, v_pk_last)
                te.wait_ge(s_w, w_wo)
                if v_att_evac:
                    te.wait_ge(s_v, v_att_evac)
                for hp in range(8):
                    te.matmul(bank[:, :], OF[:, hp, tc * 128:(tc + 1) * 128],
                              WB[:, 0, hp, half * 512:(half + 1) * 512], start=(hp == 0), stop=False)
                self.inc(te.matmul(bank[:, :], ONB[0:1, 0:128], WOBS[0:1, half * 512:(half + 1) * 512],
                                   start=False, stop=True), s_t, 1)
                ve.wait_ge(s_t, self.val(s_t))
                if tc >= 2 and half == 0:
                    ve.wait_ge(s_att, 16 * (tc - 1))
                self.inc(ve.tensor_copy(ATS[:, tc % 2, half * 512:(half + 1) * 512], bank[:, :]), s_v, 1)
                v_att_evac = self.val(s_v)
            ac.wait_ge(s_v, v_att_evac)
            self.inc(ac.dma_start(out=att_d.ap()[tc * 128:(tc + 1) * 128, :], in_=ATS[:, tc % 2, :]), s_att, 16)
        att_all = self.val(s_att)
        t_oproj = self.val(s_t)
        self.tap("att", ATS[:, (NT - 1) % 2, :], (s_v, v_att_evac))

        gp.wait_ge(s_sc, RS["sc_scat"])
        gp.wait_ge(s_att, att_all)
        for el in range(EPC):
            base = el * (CAP + 1)
            self.inc(gp.dma_start(out=TRD[:, el, :],
                                  in_=tbl.ap()[base:base + 1024, 0:1].rearrange("(bb a) o -> a (bb o)", a=16),
                                  allow_slow_non_contiguous=True), s_g, 16)
        g_trd = self.val(s_g)
        ve.wait_ge(s_g, g_trd)
        trdf = TRD[:, :, :].rearrange("a e b -> a (e b)")
        ve.tensor_scalar(trdf, trdf, 1.0, None, ALU.subtract)
        ve.drain()
        ve.tensor_scalar(SF1[:, 0:128], trdf, 0.0, None, ALU.is_lt)
        ve.drain()
        ve.scalar_tensor_tensor(SF2[:, 0:128], SF1[:, 0:128], 16385.0, trdf, ALU.mult, ALU.add)
        ve.drain()
        self.inc(ve.tensor_copy(GIX[0:16, :, :].rearrange("a e b -> a (e b)"), SF2[:, 0:128]), s_v, 1)
        v_gix = self.val(s_v)
        sync.wait_ge(s_v, v_gix)
        for r in range(1, 8):
            self.inc(sync.dma_start(out=GIX[r * 16:(r + 1) * 16, :, :], in_=GIX[0:16, :, :]), s_d, 16)
        d_gix = self.val(s_d)

        self.tap("oht", OHT[:, :], (s_v, RS["v_excl"]))
        self.tap("excl", EXCL[:, :], (s_v, RS["v_excl"]))
        self.tap("loco", LOCO[:, :], (s_v, RS["v_loco"]))
        self.tap("lgo", LGO[:, :], (s_v, RS["v_padi"]))
        self.tap("gvo", GVO[:, :], (s_v, RS["v_padi"]))
        self.tap("cio", CIO[:, :], (s_v, RS["v_padi"]))
        self.tap("slw", SLW[0:16, :], (s_v, RS["v_slot"]))
        self.tap("gix", GIX[0:16, :, :].rearrange("a e b -> a (e b)"), (s_v, v_gix))
        self.tap("ciw", CIW[0:16, :], (s_g, RS["g_ciw"]))

        if _STOP_AFTER == "R":
            self._finish(y, VB, s_y)
            return


        if _STOP_AFTER == "B":
            self._finish(y, VB, s_y)
            return

        dfm = [QT, KT, QR, KR]
        for i in range(1, 4):
            assert self.nc.lookup_mloc(dfm[i]).addr - self.nc.lookup_mloc(QT).addr == 4096 * i
        DFM8 = self.alias("DFM8", QT, 0, [128, 8, 1024], BF16)

        def dfm_ap(k, lo, hi):
            return DFM8[:, k, lo:hi]

        def cap_ap(i):
            if i < 2:
                return SD0[:, i, :]
            if i < 4:
                return SD1[:, i - 2, :]
            return PS[i][:, :]

        dispv = VB[:, :, :].rearrange("p a b -> p (a b)")
        gp.wait_ge(s_cc, cc_xn)
        gp.wait_ge(s_d, d_gix)
        gp.wait_ge(s_t, t_attn_end)
        for el in range(EPC):
            self.inc(gp.dma_gather(dispv[:, el * 8192:(el + 1) * 8192].rearrange("p (cc x) -> p cc x", x=1024),
                                   xn_glob.ap(), GIX[:, el, :], CAP, CAP, 1024), s_g, 16)
        g_disp = self.val(s_g)

        t_htC = t_oproj
        t_ef_prev = None
        for el in range(EPC):
            sync.wait_ge(s_g, g_disp)
            sync.wait_ge(s_att, att_all)
            sync.wait_ge(s_v, v_att_evac)
            if el == 1:
                sync.wait_ge(s_t, t_htC)
            for cch in range(8):
                self.inc(sync.dma_start(out=DFM8[:, :, cch * 128:(cch + 1) * 128],
                                        in_=dispv[:, el * 8192 + cch * 1024:el * 8192 + (cch + 1) * 1024],
                                        transpose=True), s_ot, 16)
            ot_disp = self.val(s_ot)
            hps = [SD0[:, 0, :], SD0[:, 1, :], SD1[:, 0, :], SD1[:, 1, :]]
            a_expall = self.val(s_a)
            te.wait_ge(s_a, a_expall)
            if el == 1:
                te.wait_ge(s_v, self.val(s_v))
            t_hf = {}
            a_rl = {}
            w1v = {}

            def w1load(f):
                pbw = f % 2
                if f >= 2:
                    gp.wait_ge(s_t, t_hf[f - 2])
                self.inc(gp.dma_start(out=W1T[:, pbw, :, :],
                                      in_=w1_d[el].ap()[:, f * 128:(f + 1) * 128].rearrange("(k p) o -> p k o", p=128)),
                         s_w, 16)
                w1v[f] = self.val(s_w)

            w1load(0)
            w1load(1)
            for f in range(32):
                pbw = f % 2
                te.wait_ge(s_w, w1v[f])
                if f == 0:
                    te.wait_ge(s_ot, ot_disp)
                for half in range(2):
                    hb = hps[2 * pbw + half]
                    if f >= 2:
                        te.wait_ge(s_a, a_rl[(f - 2, half)])
                    mmh = None
                    for k in range(NKC):
                        mmh = te.matmul(hb, W1T[:, pbw, k, :], dfm_ap(k, half * 512, (half + 1) * 512),
                                        start=(k == 0), stop=(k == NKC - 1))
                    self.inc(mmh, s_t, 1)
                    if half == 1:
                        t_hf[f] = self.val(s_t)
                    ac.wait_ge(s_t, self.val(s_t))
                    HT = QF if f < 16 else KF
                    fo = f if f < 16 else f - 16
                    htv = HT[:, :, :].rearrange("p a b -> p (a b)")
                    self.inc(ac.activation(htv[:, fo * 1024 + half * 512:fo * 1024 + (half + 1) * 512],
                                           hb, AF.Relu), s_a, 1)
                    a_rl[(f, half)] = self.val(s_a)
                if f + 2 < 32:
                    w1load(f + 2)
            a_ht = self.val(s_a)
            t_htC = self.val(s_t)
            for dmh in range(2):
                v_eo_evac = self.val(s_v)
                t_ef = {}
                w2v = {}

                def w2load(f):
                    pbw = f % 2
                    if f >= 2:
                        gp.wait_ge(s_t, t_ef[f - 2])
                    elif t_ef_prev is not None:
                        gp.wait_ge(s_t, t_ef_prev[f + 30])
                    self.inc(gp.dma_start(out=W2T[:, pbw, 0:512],
                                          in_=w2_d[el].ap()[f * 128:(f + 1) * 128, dmh * 512:(dmh + 1) * 512]),
                             s_w, 16)
                    w2v[f] = self.val(s_w)

                w2load(0)
                w2load(1)
                for f in range(32):
                    pbw = f % 2
                    te.wait_ge(s_w, w2v[f])
                    te.wait_ge(s_a, a_ht)
                    if f == 0:
                        te.wait_ge(s_v, v_eo_evac)
                    tmm = None
                    for cap_c in range(8):
                        HTT = QF if f < 16 else KF
                        fo = f if f < 16 else f - 16
                        htv = HTT[:, :, :].rearrange("p a b -> p (a b)")
                        tmm = te.matmul(cap_ap(cap_c),
                                        htv[:, fo * 1024 + cap_c * 128:fo * 1024 + (cap_c + 1) * 128],
                                        W2T[:, pbw, 0:512],
                                        start=(f == 0), stop=(f == 31))
                    self.inc(tmm, s_t, 1)
                    t_ef[f] = self.val(s_t)
                    if f + 2 < 32:
                        w2load(f + 2)
                t_ef_prev = t_ef
                t_eo = self.val(s_t)
                ve.wait_ge(s_t, t_eo)
                for cap_c in range(8):
                    pbe = cap_c % 2
                    if self.val(s_eo) >= 32:
                        ve.wait_ge(s_eo, self.val(s_eo) - 16)
                    self.inc(ve.tensor_copy(EOS[:, pbe, 0:512], cap_ap(cap_c)), s_v, 1)
                    ac.wait_ge(s_v, self.val(s_v))
                    self.inc(ac.dma_start(out=eo_self.ap().rearrange("r d -> (r d)")
                                          [(el * CAP + cap_c * 128) * D + dmh * 512:]
                                          [0:128 * D - dmh * 512].rearrange("(p d) -> p d", d=D)[:, 0:512]
                                          if False else
                                          eo_self.ap()[el * CAP + cap_c * 128:el * CAP + (cap_c + 1) * 128,
                                                       dmh * 512:(dmh + 1) * 512],
                                          in_=EOS[:, pbe, 0:512]), s_eo, 16)
        eo_all = self.val(s_eo)
        t_expert_end = self.val(s_t)

        if _STOP_AFTER == "C":
            self._finish(y, VB, s_y)
            return

        gp.wait_ge(s_eo, eo_all)
        self.inc(gp.collective_compute("AllGather", ALU.bypass, replica_groups=[list(range(NCORES))],
                                       ins=[eo_self.ap().opt()], outs=[eo_glob.ap().opt()]), s_cc, 1)
        gp.wait_ge(s_cc, self.val(s_cc))
        gp.wait_ge(s_g, RS["g_ciw"])
        gp.wait_ge(s_t, t_expert_end)
        gfull = QF[:, :, :].rearrange("p a b -> p (a b)")
        if _STOP_AFTER == "D0":
            self._finish(y, VB, s_y)
            return
        g_half = []
        for gh in range(2):
            self.inc(gp.dma_gather(gfull[:, gh * 8192:(gh + 1) * 8192].rearrange("p (cc x) -> p cc x", x=1024),
                                   eo_glob.ap(), CIW[:, gh * 64:(gh + 1) * 64], 1024, 1024, 1024), s_g, 16)
            g_half.append(self.val(s_g))
        g_comb = self.val(s_g)
        if _STOP_AFTER == "D1":
            gp.wait_ge(s_g, g_comb)
            self._finish(y, VB, s_y)
            return

        sync.wait_ge(s_t, t_oproj)
        self.inc(sync.dma_start(out=XT[:, 0, :], in_=xin.ap()[0:128, :]), s_d, 16)
        sync.wait_ge(s_t, t_expert_end)
        self.inc(sync.dma_start(out=ATD[:, 0, :], in_=att_d.ap()[0:128, :]), s_d, 16)
        d_base = self.val(s_d)
        v_fin = 0
        for ct in range(NT):
            pb = ct % 2
            if ct + 1 < NT:
                if ct >= 1:
                    sync.wait_ge(s_v, v_fin)
                self.inc(sync.dma_start(out=XT[:, (ct + 1) % 2, :], in_=xin.ap()[(ct + 1) * 128:(ct + 2) * 128, :]), s_d, 16)
                self.inc(sync.dma_start(out=ATD[:, (ct + 1) % 2, :], in_=att_d.ap()[(ct + 1) * 128:(ct + 2) * 128, :]), s_d, 16)
            ve.wait_ge(s_g, g_half[ct // 8])
            ve.wait_ge(s_d, d_base + 32 * ct)
            if ct >= 2:
                ve.wait_ge([s_y, s_ya][ct % 2], 16 * (ct // 2))
            ve.tensor_scalar(XNT[:, pb, :], gfull[:, ct * 1024:(ct + 1) * 1024], GVO[:, ct:ct + 1], None, ALU.mult)
            ve.drain()
            ve.tensor_tensor(XNT[:, pb, :], XNT[:, pb, :], ATD[:, pb, :], ALU.add)
            ve.drain()
            self.inc(ve.tensor_tensor(XNT[:, pb, :], XNT[:, pb, :], XT[:, pb, :], ALU.add), s_v, 1)
            v_fin = self.val(s_v)
            qeng, qsem = [(ac, s_y), (sync, s_ya)][ct % 2]
            qeng.wait_ge(s_v, v_fin)
            self.inc(qeng.dma_start(out=y.ap()[ct * 128:(ct + 1) * 128, :], in_=XNT[:, pb, :]), qsem, 16)
        ac.wait_ge(s_y, self.val(s_y))
        ac.wait_ge(s_ya, self.val(s_ya))
        if self.semc[self.s_tap.name]:
            ac.wait_ge(self.s_tap, self.semc[self.s_tap.name])

    def _prev(self, dd, hp, qb, dbuf):
        best = 0
        for (h2, q2, k2), v in dd.items():
            if (h2, q2) < (hp, qb) and k2 % 2 == dbuf:
                best = max(best, v)
        return best

    def _prevpk(self, v_pk, hp, qb):
        best = 0
        for (h2, q2), v in v_pk.items():
            if (h2, q2) < (hp, qb):
                best = max(best, v)
        return best

    def _prevpk_par(self, v_pk, hp, qb):
        best = 0
        for (h2, q2), v in v_pk.items():
            if (h2, q2) < (hp, qb) and q2 % 2 == qb % 2:
                best = max(best, v)
        return best

    def _finish(self, y, VB, s_y):
        sync = self.nc.sync
        if self.semc[self.s_tap.name]:
            sync.wait_ge(self.s_tap, self.semc[self.s_tap.name])
        self.inc(self.nc.gpsimd.dma_start(out=y.ap()[0:128, :], in_=VB[:, 0, 0:1024]), s_y, 16)
        self.nc.gpsimd.wait_ge(s_y, self.val(s_y))


def _get_prog():
    key = (_STOP_AFTER, tuple(sorted(_TAPS)))
    if key not in _CACHE:
        k = _K()
        with k.ctx:
            k.build()
            k.nc.compile()
        _CACHE[key] = k
    return _CACHE[key]


def kernel(**inputs):
    in_maps = _hostprep(inputs)
    k = _get_prog()
    res = run_bass_kernel_spmd(k.nc, in_maps, core_ids=list(range(NCORES)))
    kernel._last = res
    out = np.stack([np.asarray(res.results[c]["y"]) for c in range(NCORES)], axis=0)
    return out.astype(np.float32)


# version 10
# speedup vs baseline: 1.0099x; 1.0099x over previous
import numpy as np
from contextlib import ExitStack

import ml_dtypes
import concourse.bacc as bacc
import concourse.mybir as mybir
from concourse.bass_utils import run_bass_kernel_spmd
from concourse.library_config import mlp as _MLP_LIB

F32 = mybir.dt.float32
BF16 = mybir.dt.bfloat16
I16 = mybir.dt.int16
AF = mybir.ActivationFunctionType
ALU = mybir.AluOpType
AX = mybir.AxisListType

NCORES = 8
S = 2048
D = 1024
H = 16
HD = 64
NT = S // 128
NE = 16
EPC = 2
CAP = 1024
FF = 4096
NKC = D // 128
SHIFT = 18.0
NEG = -30000.0
TBL_ROWS = EPC * (CAP + 1) + 2 + 16384
GROWS = 16384 + 1

_TAPS = set()
_STOP_AFTER = None
_CACHE = {}


def _hostprep(inputs):
    x = np.ascontiguousarray(np.asarray(inputs["L0_attn_x"], dtype=np.float32))
    wq = np.asarray(inputs["L0_attn_wq"], dtype=np.float32)
    wk = np.asarray(inputs["L0_attn_wk"], dtype=np.float32)
    wv = np.ascontiguousarray(np.asarray(inputs["L0_attn_wv"], dtype=np.float32))
    wo = np.ascontiguousarray(np.asarray(inputs["L0_attn_wo_w"], dtype=np.float32))
    wob = np.ascontiguousarray(np.asarray(inputs["L0_attn_wo_b"], dtype=np.float32).reshape(1, D))
    wg = np.ascontiguousarray(np.asarray(inputs["L0_ffn_Wg"], dtype=np.float32))
    w1 = np.asarray(inputs["L0_ffn_w1"], dtype=np.float32)
    w2 = np.asarray(inputs["L0_ffn_w2"], dtype=np.float32)

    perm = np.empty(D, dtype=np.int64)
    for h in range(H):
        for f in range(32):
            perm[h * 32 + f] = h * 64 + 2 * f
            perm[512 + h * 32 + f] = h * 64 + 2 * f + 1
    wqp = np.ascontiguousarray(wq[:, perm])
    wkp = np.ascontiguousarray(wk[:, perm])

    inv = (1.0 / (10000.0 ** (np.arange(0, HD, 2, dtype=np.float32)[: HD // 2] / np.float32(HD)))).astype(np.float32)
    angf = (np.arange(S, dtype=np.float32)[:, None] * inv[None, :]).astype(np.float32)
    cos = np.cos(angf).astype(np.float32)
    sin = np.sin(angf).astype(np.float32)
    csrep = np.zeros((S, D), np.float32)
    for h in range(H):
        csrep[:, h * 32: h * 32 + 32] = cos
        csrep[:, 512 + h * 32: 512 + h * 32 + 32] = sin

    idf = np.eye(128, dtype=np.float32)
    tri = np.zeros((128, 128), np.float32)
    for k in range(128):
        tri[k, :k] = NEG
    trib = np.asarray(tri.astype(ml_dtypes.bfloat16))
    onesb = np.asarray(np.ones((1, 128), ml_dtypes.bfloat16))
    idb = np.asarray(np.eye(128, dtype=np.float32).astype(ml_dtypes.bfloat16))
    onesf = np.ones((128, 128), np.float32)
    iota16r = np.tile(np.arange(16, dtype=np.float32)[None, :], (128, 1))
    iota16p = iota16r + 100.0
    iotac16 = np.arange(16, dtype=np.float32).reshape(16, 1)
    p = np.arange(128)
    c = np.arange(128)
    g = (p[:, None] % 16) * 1024 + c[None, :] * 8 + p[:, None] // 16
    gpad = np.zeros((128, 192), np.float32)
    gpad[:, :128] = g + 1.0
    gpd64 = np.zeros((512, 2048), np.float32)
    for qq in range(4):
        gpd64[qq * 128:(qq + 1) * 128] = np.repeat(gpad[:, qq * 32:(qq + 1) * 32], 64, axis=1)
    a = np.arange(16)
    bb = np.arange(1024)
    gwrap = (2052.0 + a[:, None] * 1024 + bb[None, :]).astype(np.float32)
    sels = np.zeros((16, 16, 16), np.float32)
    for ct in range(16):
        sels[ct, ct, :] = 1.0
    sels = sels.transpose(1, 0, 2).reshape(16, 256)

    common = dict(wq=wqp, wk=wkp, wv=wv, wo=wo, wob=wob, wg=wg, cs=csrep,
                  idf=idf, trib=trib, onesb=onesb, idb=idb, onesf=onesf,
                  iota16r=iota16r, iota16p=iota16p, iotac16=iotac16,
                  gpad=gpad, gpd64=gpd64, gwrap=gwrap, sels=sels)
    in_maps = []
    for b in range(NCORES):
        m = dict(common)
        m["xin"] = np.ascontiguousarray(x[b])
        m["w1a"] = np.ascontiguousarray(w1[b])
        m["w1b"] = np.ascontiguousarray(w1[b + 8])
        m["w2a"] = np.ascontiguousarray(w2[b])
        m["w2b"] = np.ascontiguousarray(w2[b + 8])
        pcv = np.zeros((128, 2), np.float32)
        pcv[:, 0] = b * S
        pcv[:, 1] = b
        m["pcv"] = pcv
        mexp = np.zeros((128, 16), np.float32)
        mexp[:, 0:8] = (np.arange(8)[None, :] < b).astype(np.float32)
        m["maskexp"] = mexp
        in_maps.append(m)
    return in_maps


class _K:
    def __init__(self):
        self.nc = bacc.Bacc("TRN2", target_bir_lowering=False, debug=False, detect_race_conditions=False)
        self.ctx = ExitStack()
        self.semc = {}
        self.tapped = []

    def sem(self, name):
        s = self.ctx.enter_context(self.nc.semaphore(name))
        self.semc[name] = 0
        return s

    def inc(self, instr, s, n=1):
        instr.then_inc(s, n)
        self.semc[s.name] += n
        return self.semc[s.name]

    def val(self, s):
        return self.semc[s.name]

    def sb(self, name, shape, dt):
        return self.ctx.enter_context(self.nc.sbuf_tensor(name, shape, dt))

    def alias(self, name, target, delta, shape, dt):
        base = self.nc.lookup_mloc(target).addr
        return self.nc.alloc_sbuf_tensor_at(name, shape, dt, offset=base + delta)

    def ps(self, name):
        return self.ctx.enter_context(self.nc.psum_tensor(name, [128, 512], F32))

    def dram(self, name, shape, dt, kind="Internal", shared=False):
        return self.nc.dram_tensor(name, list(shape), dt, kind=kind,
                                   addr_space="Shared" if shared else "Local")

    def tap(self, name, ap, wait):
        if name not in _TAPS:
            return
        t = self.nc.dram_tensor(f"tap_{name}", list(ap.shape), ap.dtype, kind="ExternalOutput")
        self.nc.scalar.wait_ge(wait[0], wait[1])
        self.inc(self.nc.scalar.dma_start(out=t.ap(), in_=ap), self.s_tap, 16)
        self.tapped.append(name)

    def overlap64(self, t):
        import dataclasses
        base = t[:, :]
        newap = [list(base.ap)[0], [1, 128], [1, 64]]
        return dataclasses.replace(base, ap=newap)

    def build(self):
        nc = self.nc
        sync, gp, ve, te, ac = nc.sync, nc.gpsimd, nc.vector, nc.tensor, nc.scalar

        xin = self.dram("xin", [S, D], F32, "ExternalInput")
        y = self.dram("y", [S, D], F32, "ExternalOutput")
        wq_d = self.dram("wq", [D, D], F32, "ExternalInput")
        wk_d = self.dram("wk", [D, D], F32, "ExternalInput")
        wv_d = self.dram("wv", [D, D], F32, "ExternalInput")
        wo_d = self.dram("wo", [D, D], F32, "ExternalInput")
        wob_d = self.dram("wob", [1, D], F32, "ExternalInput")
        wg_d = self.dram("wg", [D, NE], F32, "ExternalInput")
        w1_d = [self.dram("w1a", [D, FF], F32, "ExternalInput"),
                self.dram("w1b", [D, FF], F32, "ExternalInput")]
        w2_d = [self.dram("w2a", [FF, D], F32, "ExternalInput"),
                self.dram("w2b", [FF, D], F32, "ExternalInput")]
        cs_d = self.dram("cs", [S, D], F32, "ExternalInput")
        idf_d = self.dram("idf", [128, 128], F32, "ExternalInput")
        trib_d = self.dram("trib", [128, 128], BF16, "ExternalInput")
        onesb_d = self.dram("onesb", [1, 128], BF16, "ExternalInput")
        idb_d = self.dram("idb", [128, 128], BF16, "ExternalInput")
        onesf_d = self.dram("onesf", [128, 128], F32, "ExternalInput")
        iota16r_d = self.dram("iota16r", [128, 16], F32, "ExternalInput")
        iota16p_d = self.dram("iota16p", [128, 16], F32, "ExternalInput")
        iotac16_d = self.dram("iotac16", [16, 1], F32, "ExternalInput")
        gpad_d = self.dram("gpad", [128, 192], F32, "ExternalInput")
        gpd64_d = self.dram("gpd64", [512, 2048], F32, "ExternalInput")
        gwrap_d = self.dram("gwrap", [16, 1024], F32, "ExternalInput")
        pcv_d = self.dram("pcv", [128, 2], F32, "ExternalInput")
        maskexp_d = self.dram("maskexp", [128, 16], F32, "ExternalInput")
        sels_d = self.dram("sels", [16, 256], F32, "ExternalInput")

        xn_self = self.dram("xn_self", [S, D], BF16)
        xn_glob = self.dram("xn_glob", [GROWS, D], BF16, "Internal", shared=True)
        att_d = self.dram("att_d", [S, D], BF16)
        cnt_b = self.dram("cnt_b", [16, 1], F32)
        cnt_all = self.dram("cnt_all", [128, 1], F32, shared=True)
        b2i = self.dram("b2i", [2, 1024], I16)
        b2l = self.dram("b2l", [2, 1024], I16)
        rti = self.dram("rti", [16, 1024], I16, shared=True)
        rtl = self.dram("rtl", [16, 1024], I16, shared=True)
        ci_d = self.dram("ci_d", [16, 128], I16)
        tbl = self.dram("tbl", [TBL_ROWS, 64], F32)
        eo_self = self.dram("eo_self", [EPC * CAP, D], BF16)
        eo_glob = self.dram("eo_glob", [NCORES * EPC * CAP, D], BF16, shared=True)

        QF = self.sb("QF", [128, NKC, S], BF16)
        KF = self.sb("KF", [128, NKC, S], BF16)
        VB = self.sb("VB", [128, NT, H * 65], BF16)
        OF = self.sb("OF", [128, NKC, S], BF16)
        WB = self.sb("WB", [128, 3, NKC, D], BF16)
        QT = self.sb("QT", [128, 2, D], BF16)
        KT = self.sb("KT", [128, 2, D], BF16)
        QR = self.sb("QR", [128, 2, D], BF16)
        KR = self.sb("KR", [128, 2, D], BF16)
        T1 = self.sb("T1", [128, 512], BF16)
        T2 = self.sb("T2", [128, 512], BF16)
        T3 = self.sb("T3", [128, 512], BF16)
        T4 = self.sb("T4", [128, 512], BF16)
        PVT2 = self.sb("PVT2", [128, 16], F32)
        SM3 = self.sb("SM3", [128, 16], F32)
        EB = self.sb("EB", [128, 2, 1024], BF16)
        PKD = self.sb("PKD", [128, 2, 256], BF16)
        LNS = self.sb("LNS", [128, 8], F32)
        WGS = self.sb("WGS", [128, NKC, NE], F32)
        RBI = self.sb("RBI", [128, 16], F32)
        RBG = self.sb("RBG", [128, 16], F32)
        GTS = self.sb("GTS", [128, 16], F32)
        IDXR = self.sb("IDXR", [16, 128], F32)
        CNT = self.sb("CNT", [16, 1], F32)
        CALL16 = self.sb("CALL16", [16, 8], F32)
        PRV = self.sb("PRV", [16, 1], F32)
        PRVB = self.sb("PRVB", [128, 16], F32)
        LOCO = self.sb("LOCO", [128, 16], F32)
        PVT = self.sb("PVT", [128, 16], F32)
        LGO = self.sb("LGO", [128, 16], F32)
        GVO = self.sb("GVO", [128, 16], F32)
        CIO = self.sb("CIO", [128, 16], F32)
        SM1 = self.sb("SM1", [128, 4], F32)
        SM2 = self.sb("SM2", [128, 16], F32)
        PADI = self.sb("PADI", [128, 128], I16)
        TPI = self.sb("TPI", [128, 128], I16)
        GPD = self.sb("GPD", [128, 192], F32)
        CIW = self.sb("CIW", [128, 128], I16)
        TRD = self.sb("TRD", [16, 2, 64], F32)
        GIX = self.sb("GIX", [128, 2, 64], I16)
        RLX = self.sb("RLX", [128, 2], F32)
        NEGSH = self.sb("NEGSH", [128, 1], F32)
        IDF = self.sb("IDF", [128, 128], F32)
        TRI = self.sb("TRI", [128, 128], BF16)
        ONB = self.sb("ONB", [1, 128], BF16)
        IDB = self.sb("IDB", [128, 128], BF16)
        ONF = self.sb("ONF", [128, 128], F32)
        I16R = self.sb("I16R", [128, 16], F32)
        I16P = self.sb("I16P", [128, 16], F32)
        IC16 = self.sb("IC16", [16, 1], F32)
        MEXP = self.sb("MEXP", [128, 16], F32)
        PCV = self.sb("PCV", [128, 2], F32)

        XT = self.alias("XT", OF, 0, [128, 2, D], F32)
        XNT = self.alias("XNT", OF, 8192, [128, 2, D], F32)
        XF32 = self.alias("XF32", OF, 16384, [128, 2, D], F32)
        XFB = self.alias("XFB", OF, 24576, [128, 2, D], BF16)
        CST = self.alias("CST", OF, 28672, [128, 2, D], BF16)
        SQ = self.alias("SQ", XF32, 0, [128, 2, D], F32)
        W1T = self.alias("W1T", XF32, 0, [128, 2, NKC, 128], BF16)
        W2T = self.alias("W2T", XF32, 4096, [128, 2, D], BF16)
        EOS = self.alias("EOS", XFB, 0, [128, 2, D], BF16)
        WOBS = self.alias("WOBS", QT, 0, [1, D], BF16)
        IDWLOW = self.alias("IDWLOW", WB, 0, [16, 2, 1024], I16)
        EXI = self.alias("EXI", WB, 4096, [16, S], I16)
        EXT = self.alias("EXT", T3, 0, [128, 16, 16], I16)
        EXF = self.alias("EXF", T4, 0, [128, 256], F32)
        OHT = self.alias("OHT", WB, 16384, [16, S], F32)
        INCL = self.alias("INCL", WB, 24576, [16, S], F32)
        EXCL = self.alias("EXCL", WB, 32768, [16, S], F32)
        GWR = self.alias("GWR", WB, 40960, [16, 1024], F32)
        SELS = self.alias("SELS", WB, 45056, [16, 256], F32)
        SF1 = self.alias("SF1", QT, 0, [16, 1024], F32)
        SF2 = self.alias("SF2", KT, 0, [16, 1024], F32)
        SF3 = self.alias("SF3", QR, 0, [16, 1024], F32)
        SLW = self.alias("SLW", T1, 0, [128, 1024], I16)
        GSVA = self.alias("GSVA", QT, 0, [128, 32, 64], F32)
        GSVB = self.alias("GSVB", QR, 0, [128, 32, 64], F32)
        ZB = self.alias("ZB", EB, 0, [128, 512], F32)
        ZROW = self.alias("ZROW", EB, 2048, [1, 1024], BF16)
        ATS = self.alias("ATS", KR, 0, [128, 2, D], BF16)
        ATD = self.alias("ATD", KF, 0, [128, 2, D], BF16)

        SD = [self.ctx.enter_context(self.nc.psum_tensor(f"SD{i}", [128, 2, 512], F32)) for i in range(2)]
        SD0, SD1 = SD
        PS = [None, None, None, None] + [self.ps(f"PS{i}") for i in range(4, 8)]

        s_w = self.sem("s_w")
        s_wc = self.sem("s_wc")
        s_x = [self.sem("s_x0"), self.sem("s_x1")]
        s_cs = [self.sem("s_cs0"), self.sem("s_cs1")]
        s_v = self.sem("s_v")
        s_a = self.sem("s_a")
        s_t = self.sem("s_t")
        s_qkt = self.sem("s_qkt")
        s_qf = self.sem("s_qf")
        s_xn = self.sem("s_xn")
        s_cc = self.sem("s_cc")
        s_g = self.sem("s_g")
        s_sc = self.sem("s_sc")
        s_ot = self.sem("s_ot")
        s_att = self.sem("s_att")
        s_eo = self.sem("s_eo")
        s_d = self.sem("s_d")
        s_y = self.sem("s_y")
        s_ya = self.sem("s_ya")
        self.s_tap = self.sem("s_tap")

        gp.load_library(_MLP_LIB)
        self.inc(sync.dma_start(out=XT[:, 0, :], in_=xin.ap()[0:128, :]), s_x[0], 16)
        self.inc(gp.dma_start(out=CST[:, 0, :], in_=cs_d.ap()[0:128, :]), s_cs[0], 16)
        self.inc(gp.dma_start(out=WGS[:, :, :], in_=wg_d.ap().rearrange("(k p) e -> p k e", p=128)), s_w, 16)

        for dst, src in [(IDF, idf_d), (TRI, trib_d), (ONB, onesb_d), (IDB, idb_d), (ONF, onesf_d),
                         (I16R, iota16r_d), (I16P, iota16p_d), (IC16, iotac16_d),
                         (MEXP, maskexp_d), (PCV, pcv_d), (GPD, gpad_d)]:
            self.inc(sync.dma_start(out=dst[:, :], in_=src.ap()), s_wc, 16)
        wc_all = self.val(s_wc)

        for i, wd in enumerate([wq_d, wk_d, wv_d]):
            self.inc(gp.dma_start(out=WB[:, i, :, :], in_=wd.ap().rearrange("(k p) o -> p k o", p=128)), s_w, 16)
        w_init = self.val(s_w)

        self.inc(ve.memset(ZB[:, :], 0.0), s_v, 1)
        self.inc(ve.memset(ZROW[:, :], 0.0), s_v, 1)
        self.inc(ve.memset(NEGSH[:, :], -SHIFT), s_v, 1)
        v_z = self.val(s_v)
        gp.wait_ge(s_v, v_z)
        self.inc(gp.dma_start(out=xn_glob.ap()[GROWS - 1:GROWS, :], in_=ZROW[:, :]), s_sc, 16)
        tblf = tbl.ap().rearrange("r e -> (r e)")
        total = TBL_ROWS * 64
        assert total % 128 == 0
        cols_total = total // 128
        off = 0
        while off < cols_total:
            n = min(512, cols_total - off)
            self.inc(gp.dma_start(out=tblf[off * 128:(off + n) * 128].rearrange("(p f) -> p f", p=128),
                                  in_=ZB[:, 0:n]), s_sc, 16)
            off += n
        sc_zero = self.val(s_sc)

        v_rope = [0] * NT
        v_gate = [0] * NT
        a_qk = [0] * NT
        t_proj = [0] * NT
        v_evac = [0] * NT
        qkt_t = [0] * NT
        qft_t = [0] * NT
        a_sq = [0] * NT
        a_gate = [0] * NT
        te.wait_ge(s_wc, wc_all)
        for c in range(NT):
            pb = c % 2
            if c + 1 < NT:
                if c >= 1:
                    ac.wait_ge(s_v, v_rope[c - 1])
                self.inc(ac.dma_start(out=XT[:, (c + 1) % 2, :], in_=xin.ap()[(c + 1) * 128:(c + 2) * 128, :]), s_x[(c + 1) % 2], 16)
                if c >= 1:
                    gp.wait_ge(s_v, v_rope[c - 1])
                self.inc(gp.dma_start(out=CST[:, (c + 1) % 2, :], in_=cs_d.ap()[(c + 1) * 128:(c + 2) * 128, :]), s_cs[(c + 1) % 2], 16)
            ve.wait_ge(s_x[pb], 16 * (c // 2 + 1))
            ve.tensor_reduce(LNS[:, 0:1], XT[:, pb, :], AX.X, ALU.add)
            ve.drain()
            self.inc(ve.tensor_scalar(LNS[:, 1:2], LNS[:, 0:1], 1.0 / D, None, ALU.mult), s_v, 1)
            ac.wait_ge(s_x[pb], 16 * (c // 2 + 1))
            if c >= 2:
                ac.wait_ge(s_t, t_proj[c - 2])
            self.inc(ac.activation(SQ[:, pb, :], XT[:, pb, :], AF.Square, accum_out=LNS[:, 2:3]), s_a, 1)
            a_sq[c] = self.val(s_a)
            ve.wait_ge(s_a, a_sq[c])
            ve.tensor_scalar(LNS[:, 3:4], LNS[:, 2:3], 1.0 / D, None, ALU.mult)
            ve.tensor_scalar(LNS[:, 4:5], LNS[:, 1:2], LNS[:, 1:2], None, ALU.mult)
            ve.drain()
            ve.tensor_scalar(LNS[:, 5:6], LNS[:, 3:4], LNS[:, 4:5], 1e-5, ALU.subtract, ALU.add)
            ve.drain()
            self.inc(ve.reciprocal(LNS[:, 6:7], LNS[:, 5:6]), s_v, 1)
            ac.wait_ge(s_v, self.val(s_v))
            self.inc(ac.activation(LNS[:, 7:8], LNS[:, 6:7], AF.Sqrt), s_a, 1)
            ve.wait_ge(s_a, self.val(s_a))
            if c >= 2:
                ve.wait_ge(s_t, t_proj[c - 2])
                ve.wait_ge(s_xn, 16 * (c - 1))
            self.inc(ve.tensor_scalar(XNT[:, pb, :], XT[:, pb, :], LNS[:, 1:2], LNS[:, 7:8],
                                      ALU.subtract, ALU.mult), s_v, 1)
            v_xn = self.val(s_v)
            gp.wait_ge(s_v, v_xn)
            self.inc(gp.dma_start(out=xn_self.ap()[c * 128:(c + 1) * 128, :], in_=XNT[:, pb, :]), s_xn, 16)
            ev_list = []
            for k in range(NKC):
                te.wait_ge(s_v, v_xn)
                if k >= 2:
                    te.wait_ge(s_v, ev_list[k - 2])
                elif c >= 1:
                    te.wait_ge(s_v, v_gate[c - 1])
                self.inc(te.transpose(SD0[:, k % 2, 0:128], XNT[:, pb, k * 128:(k + 1) * 128], IDF[:, :]), s_t, 1)
                ve.wait_ge(s_t, self.val(s_t))
                if k == 0:
                    ve.wait_ge(s_a, a_sq[c])
                ve.tensor_copy(XF32[:, pb, k * 128:(k + 1) * 128], SD0[:, k % 2, 0:128])
                self.inc(ve.tensor_copy(XFB[:, pb, k * 128:(k + 1) * 128], SD0[:, k % 2, 0:128]), s_v, 1)
                ev_list.append(self.val(s_v))
            v_xf = self.val(s_v)
            te.wait_ge(s_v, v_xf)
            te.wait_ge(s_w, w_init)
            if c >= 1:
                te.wait_ge(s_v, v_evac[c - 1])
                te.wait_ge(s_v, v_gate[c - 1])
            mm = None
            for k in range(NKC):
                st, sp = (k == 0), (k == NKC - 1)
                te.matmul(SD0[:, 0, 0:16], XF32[:, pb, k * 128:(k + 1) * 128], WGS[:, k, :], start=st, stop=sp)
                lhs = XFB[:, pb, k * 128:(k + 1) * 128]
                for half in range(2):
                    te.matmul(SD1[:, half, :], lhs, WB[:, 0, k, half * 512:(half + 1) * 512], start=st, stop=sp)
                    te.matmul(PS[4 + half][:, :], lhs, WB[:, 1, k, half * 512:(half + 1) * 512], start=st, stop=sp)
                    mm = te.matmul(PS[6 + half][:, :], lhs, WB[:, 2, k, half * 512:(half + 1) * 512], start=st, stop=sp)
            self.inc(mm, s_t, 1)
            t_proj[c] = self.val(s_t)
            ac.wait_ge(s_t, t_proj[c])
            if c >= 2:
                ac.wait_ge(s_v, v_rope[c - 2])
            ac.activation(QT[:, pb, 0:512], SD1[:, 0, :], AF.Copy)
            ac.activation(QT[:, pb, 512:1024], SD1[:, 1, :], AF.Copy)
            ac.activation(KT[:, pb, 0:512], PS[4][:, :], AF.Copy)
            self.inc(ac.activation(KT[:, pb, 512:1024], PS[5][:, :], AF.Copy), s_a, 1)
            a_qk[c] = self.val(s_a)
            ve.wait_ge(s_t, t_proj[c])
            vv = VB[:, c, :].rearrange("p (h e) -> p h e", e=65)
            ve.tensor_copy(vv[:, 0:8, 0:64], PS[6][:, :].rearrange("p (h e) -> p h e", e=64))
            ve.tensor_copy(vv[:, 8:16, 0:64], PS[7][:, :].rearrange("p (h e) -> p h e", e=64))
            self.inc(ve.memset(vv[:, :, 64:65], 1.0), s_v, 1)
            v_evac[c] = self.val(s_v)
            ve.wait_ge(s_wc, wc_all)
            self.inc(ve.tensor_copy(GTS[:, :], SD0[:, 0, 0:16]), s_v, 1)
            v_gate[c] = self.val(s_v)
            ve.drain()
            self.inc(ve.tensor_reduce(SM1[:, 0:1], GTS[:, :], AX.X, ALU.max, negate=True), s_v, 1)
            ac.wait_ge(s_v, self.val(s_v))
            self.inc(ac.activation(SM2[:, 0:16], GTS[:, :], AF.Exp, bias=SM1[:, 0:1],
                                   accum_out=SM1[:, 1:2]), s_a, 1)
            a_gate[c] = self.val(s_a)
            ve.wait_ge(s_a, self.val(s_a))
            ve.reciprocal(RBG[:, c:c + 1], SM1[:, 1:2])
            ve.tensor_scalar(SM1[:, 2:3], SM1[:, 0:1], -1.0, None, ALU.mult)
            ve.drain()
            ve.tensor_scalar(SM2[:, 0:16], GTS[:, :], SM1[:, 2:3], None, ALU.is_ge)
            ve.drain()
            ve.scalar_tensor_tensor(SM3[:, 0:16], SM2[:, 0:16], -100.0, I16P[:, :], ALU.mult, ALU.add)
            ve.drain()
            self.inc(ve.tensor_reduce(RBI[:, c:c + 1], SM3[:, 0:16], AX.X, ALU.min), s_v, 1)
            ve.wait_ge(s_a, a_qk[c])
            ve.wait_ge(s_cs[pb], 16 * (c // 2 + 1))
            if c >= 2:
                ve.wait_ge(s_qkt, qkt_t[c - 2])
                ve.wait_ge(s_qf, qft_t[c - 2])
            for (TSRC, TDST) in ((QT, QR), (KT, KR)):
                dst = TDST[:, pb, :].rearrange("p (h u f) -> p h u f", u=2, f=32)
                ve.tensor_tensor(T1[:, :], TSRC[:, pb, 0:512], CST[:, pb, 0:512], ALU.mult)
                ve.tensor_tensor(T2[:, :], TSRC[:, pb, 512:1024], CST[:, pb, 512:1024], ALU.mult)
                ve.tensor_tensor(T3[:, :], TSRC[:, pb, 0:512], CST[:, pb, 512:1024], ALU.mult)
                ve.tensor_tensor(T4[:, :], TSRC[:, pb, 512:1024], CST[:, pb, 0:512], ALU.mult)
                ve.tensor_tensor(dst[:, :, 0, :], T1[:, :].rearrange("p (h f) -> p h f", f=32),
                                 T2[:, :].rearrange("p (h f) -> p h f", f=32), ALU.subtract)
                last_rope = ve.tensor_tensor(dst[:, :, 1, :], T3[:, :].rearrange("p (h f) -> p h f", f=32),
                                             T4[:, :].rearrange("p (h f) -> p h f", f=32), ALU.add)
            self.inc(last_rope, s_v, 1)
            v_rope[c] = self.val(s_v)
            sync.wait_ge(s_v, v_rope[c])
            self.inc(sync.dma_start(out=QF[:, :, c * 128:(c + 1) * 128], in_=QR[:, pb, :], transpose=True), s_qf, 16)
            self.inc(sync.dma_start(out=KF[:, :, c * 128:(c + 1) * 128], in_=KR[:, pb, :], transpose=True), s_qkt, 16)
            qkt_t[c] = self.val(s_qkt)
            qft_t[c] = self.val(s_qf)
        qkt_all = self.val(s_qkt)
        qft_all = self.val(s_qf)
        vA = self.val(s_v)
        tA = t_proj[NT - 1]

        self.tap("xnf", XF32[:, (NT - 1) % 2, :], (s_v, v_rope[NT - 1]))
        self.tap("qf", QF[:, :, :].rearrange("p a b -> p (a b)"), (s_qf, qft_all))
        self.tap("kf", KF[:, :, :].rearrange("p a b -> p (a b)"), (s_qkt, qkt_all))
        self.tap("vb", VB[:, :, :].rearrange("p a b -> p (a b)"), (s_v, vA))
        self.tap("rbi", RBI[:, :], (s_v, vA))
        self.tap("rbg", RBG[:, :], (s_v, vA))

        if _STOP_AFTER == "A":
            self._finish(y, VB, s_y)
            return

        aA = self.val(s_a)
        te.wait_ge(s_v, vA)
        te.wait_ge(s_a, aA)
        self.inc(te.transpose(SD0[0:16, 0, 0:128], RBI[:, :], IDF[:, :]), s_t, 1)
        ve.wait_ge(s_t, self.val(s_t))
        self.inc(ve.tensor_copy(IDXR[0:16, :], SD0[0:16, 0, 0:128]), s_v, 1)
        sync.wait_ge(s_t, tA)
        self.inc(sync.dma_start(out=SELS[:, :], in_=sels_d.ap()), s_wc, 16)
        wc_sels = self.val(s_wc)
        te.wait_ge(s_wc, wc_sels)
        ve.wait_ge(s_t, tA)
        v_oh2 = self.val(s_v)
        for ct in range(NT):
            te.wait_ge(s_v, v_oh2)
            self.inc(te.matmul(SD0[0:16, 1, 0:128], SELS[:, ct * 16:(ct + 1) * 16], IDXR[0:16, :],
                               start=True, stop=True), s_t, 1)
            ve.wait_ge(s_t, self.val(s_t))
            self.inc(ve.tensor_scalar(OHT[:, ct * 128:(ct + 1) * 128], SD0[0:16, 1, 0:128], IC16[:, 0:1],
                                      None, ALU.is_equal), s_v, 1)
            v_oh2 = self.val(s_v)
        v_oh_final = v_oh2

        RS = {}

        def emit_R1b():
            ve.drain()
            ve.tensor_tensor_scan(INCL[:, :], ONF[0:16, 0:1].broadcast_to([16, S]), OHT[:, :], 0.0, ALU.mult, ALU.add)
            ve.drain()
            ve.tensor_copy(CNT[:, :], INCL[:, S - 1:S])
            self.inc(ve.tensor_tensor(EXCL[:, :], INCL[:, :], OHT[:, :], ALU.subtract), s_v, 1)
            v_excl = self.val(s_v)
            RS["v_excl"] = v_excl
            ve.drain()
            self.inc(ve.tensor_copy(EXI[:, :], EXCL[:, :]), s_v, 1)
            v_exi = self.val(s_v)
            sync.wait_ge(s_v, v_exi)
            self.inc(sync.dma_start(out=EXT[:, :, :], in_=EXI[:, :], transpose=True), s_qkt, 16)
            RS["qkt_ext"] = self.val(s_qkt)
            gp.wait_ge(s_v, v_excl)
            self.inc(gp.dma_start(out=cnt_b.ap(), in_=CNT[:, :]), s_g, 16)
            gp.wait_ge(s_g, self.val(s_g))
            self.inc(gp.collective_compute("AllGather", ALU.bypass, replica_groups=[list(range(NCORES))],
                                           ins=[cnt_b.ap().opt()], outs=[cnt_all.ap().opt()]), s_cc, 1)
            RS["cc_counts"] = self.val(s_cc)

        gp.wait_ge(s_xn, 16 * NT)
        self.inc(gp.collective_compute("AllGather", ALU.bypass, replica_groups=[list(range(NCORES))],
                                       ins=[xn_self.ap().opt()], outs=[xn_glob.ap()[0:16384, :].opt()]),
                 s_cc, 1)
        cc_xn = self.val(s_cc)

        def emit_R2():
            ve.wait_ge(s_qkt, RS["qkt_ext"])
            ve.tensor_copy(EXF[:, :], EXT[:, :, :].rearrange("p a b -> p (a b)"))
            ve.drain()
            for ct in range(NT):
                ve.scalar_tensor_tensor(SM2[:, 0:16], I16R[:, :], RBI[:, ct:ct + 1], EXF[:, ct * 16:(ct + 1) * 16],
                                        ALU.is_equal, ALU.mult)
                ve.drain()
                self.inc(ve.tensor_reduce(LOCO[:, ct:ct + 1], SM2[:, 0:16], AX.X, ALU.add), s_v, 1)
            RS["v_loco"] = self.val(s_v)
            gp.wait_ge(s_cc, RS["cc_counts"])
            self.inc(gp.dma_start(out=CALL16[:, :],
                                  in_=cnt_all.ap().rearrange("(r e) o -> e (r o)", e=16),
                                  allow_slow_non_contiguous=True), s_g, 16)
            g_call = self.val(s_g)
            ve.wait_ge(s_g, g_call)
            ve.tensor_tensor(SM3[0:16, 0:8], CALL16[:, :], MEXP[0:16, 0:8], ALU.mult)
            ve.drain()
            self.inc(ve.tensor_reduce(PRV[:, :], SM3[0:16, 0:8], AX.X, ALU.add), s_v, 1)
            sync.wait_ge(s_v, self.val(s_v))
            self.inc(sync.dma_start(out=PRVB[0:1, 0:16], in_=PRV[:, :], allow_slow_non_contiguous=True), s_d, 16)
            r = 1
            while r < 128:
                self.inc(sync.dma_start(out=PRVB[r:2 * r, :], in_=PRVB[0:r, :]), s_d, 16)
                r *= 2
            d_prvb = self.val(s_d)
            ve.wait_ge(s_d, d_prvb)
            ve.memset(PVT[:, :], 0.0)
            ve.memset(PVT2[:, :], 0.0)
            for e in range(0, NE, 2):
                ve.scalar_tensor_tensor(SM2[:, 0:16], RBI[:, :], float(e),
                                        PRVB[:, e:e + 1].broadcast_to([128, 16]), ALU.is_equal, ALU.mult)
                ve.scalar_tensor_tensor(SM3[:, 0:16], RBI[:, :], float(e + 1),
                                        PRVB[:, e + 1:e + 2].broadcast_to([128, 16]), ALU.is_equal, ALU.mult)
                ve.tensor_tensor(PVT[:, :], PVT[:, :], SM2[:, 0:16], ALU.add)
                ve.tensor_tensor(PVT2[:, :], PVT2[:, :], SM3[:, 0:16], ALU.add)
            ve.drain()
            ve.tensor_tensor(PVT[:, :], PVT[:, :], PVT2[:, :], ALU.add)
            ve.wait_ge(s_v, RS["v_loco"])
            ve.drain()
            ve.tensor_tensor(LGO[:, :], LOCO[:, :], PVT[:, :], ALU.add)
            ve.drain()
            ve.tensor_scalar(SM2[:, 0:16], LGO[:, :], float(CAP), None, ALU.is_lt)
            ve.tensor_scalar(SM3[:, 0:16], LGO[:, :], float(CAP - 1), None, ALU.min)
            ve.drain()
            ve.tensor_tensor(GVO[:, :], RBG[:, :], SM2[:, 0:16], ALU.mult)
            self.inc(ve.scalar_tensor_tensor(CIO[:, :], RBI[:, :], float(CAP), SM3[:, 0:16], ALU.mult, ALU.add), s_v, 1)
            ve.memset(PADI[:, :], 0)
            ve.drain()
            ve.tensor_copy(PADI[:, 0:16], RBI[:, :])
            ve.tensor_copy(PADI[:, 16:32], LGO[:, :])
            self.inc(ve.tensor_copy(PADI[:, 32:48], CIO[:, :]), s_v, 1)
            RS["v_padi"] = self.val(s_v)
            sync.wait_ge(s_v, RS["v_padi"])
            self.inc(sync.dma_start_transpose(TPI[:, :], PADI[:, :]), s_qkt, 16)
            sync.wait_ge(s_qkt, self.val(s_qkt))
            self.inc(sync.dma_start(out=b2i.ap().rearrange("h (jo ji) -> (h jo) ji", ji=128), in_=TPI[0:16, :]), s_d, 16)
            self.inc(sync.dma_start(out=b2l.ap().rearrange("h (jo ji) -> (h jo) ji", ji=128), in_=TPI[16:32, :]), s_d, 16)
            self.inc(sync.dma_start(out=ci_d.ap(), in_=TPI[32:48, :]), s_d, 16)
            d_bnd = self.val(s_d)
            gp.wait_ge(s_d, d_bnd)
            self.inc(gp.collective_compute("AllGather", ALU.bypass, replica_groups=[list(range(NCORES))],
                                           ins=[b2i.ap().opt()], outs=[rti.ap().opt()]), s_cc, 1)
            self.inc(gp.collective_compute("AllGather", ALU.bypass, replica_groups=[list(range(NCORES))],
                                           ins=[b2l.ap().opt()], outs=[rtl.ap().opt()]), s_cc, 1)
            RS["cc_ag2"] = self.val(s_cc)

        def emit_R3():
            gp.wait_ge(s_cc, RS["cc_ag2"])
            gp.wait_ge(s_t, tA)
            self.inc(gp.dma_start(out=IDWLOW[:, 0, :], in_=rti.ap()), s_g, 16)
            self.inc(gp.dma_start(out=IDWLOW[:, 1, :], in_=rtl.ap()), s_g, 16)
            g_wr = self.val(s_g)
            self.inc(gp.dma_start(out=CIW[0:16, :],
                                  in_=ci_d.ap().rearrange("ct p -> (ct p)").rearrange("(j a) -> a j", a=16),
                                  allow_slow_non_contiguous=True), s_g, 16)
            gp.wait_ge(s_g, self.val(s_g))
            for r in range(1, 8):
                self.inc(gp.dma_start(out=CIW[r * 16:(r + 1) * 16, :], in_=CIW[0:16, :]), s_g, 16)
            RS["g_ciw"] = self.val(s_g)
            sync.wait_ge(s_t, tA)
            self.inc(sync.dma_start(out=GWR[:, :], in_=gwrap_d.ap()), s_wc, 16)
            wc_gwr = self.val(s_wc)
            ve.wait_ge(s_g, g_wr)
            ve.wait_ge(s_wc, wc_gwr)
            IDW = IDWLOW[0:16, 0, :]
            LOW = IDWLOW[0:16, 1, :]
            ve.tensor_scalar(SF1[:, :], IDW, PCV[0:16, 1:2], None, ALU.subtract)
            ve.drain()
            ve.tensor_scalar(SF2[:, :], SF1[:, :], 0.0, None, ALU.is_equal)
            ve.tensor_scalar(SF3[:, :], SF1[:, :], 8.0, None, ALU.is_equal)
            ve.drain()
            ve.tensor_scalar(SF1[:, :], SF3[:, :], float(CAP + 1), None, ALU.mult)
            ve.tensor_tensor(SF2[:, :], SF2[:, :], SF3[:, :], ALU.add)
            ve.tensor_scalar(SF3[:, :], LOW, float(CAP), None, ALU.min)
            ve.drain()
            ve.tensor_tensor(SF1[:, :], SF1[:, :], SF3[:, :], ALU.add)
            ve.drain()
            ve.tensor_tensor(SF1[:, :], SF1[:, :], SF2[:, :], ALU.mult)
            ve.tensor_scalar(SF2[:, :], SF2[:, :], -1.0, 1.0, ALU.mult, ALU.add)
            ve.drain()
            ve.tensor_tensor(SF2[:, :], SF2[:, :], GWR[:, :], ALU.mult)
            ve.drain()
            ve.tensor_tensor(SF1[:, :], SF1[:, :], SF2[:, :], ALU.add)
            ve.drain()
            self.inc(ve.tensor_copy(SLW[0:16, :], SF1[:, :]), s_v, 1)
            v_slot = self.val(s_v)
            RS["v_slot"] = v_slot
            sync.wait_ge(s_v, v_slot)
            for r in range(1, 8):
                self.inc(sync.dma_start(out=SLW[r * 16:(r + 1) * 16, :], in_=SLW[0:16, :]), s_d, 16)
            d_slw = self.val(s_d)
            gp.wait_ge(s_d, d_slw)
            gp.wait_ge(s_sc, sc_zero)
            gp.wait_ge(s_wc, wc_all)
            scat_v = {}
            for qq in range(4):
                buf = [GSVA, GSVB][qq % 2]
                if qq >= 2:
                    gp.wait_ge(s_sc, scat_v[qq - 2])
                self.inc(gp.dma_start(out=buf[:, :, :],
                                      in_=gpd64_d.ap()[qq * 128:(qq + 1) * 128, :]
                                      .rearrange("p (c e) -> p c e", e=64)), s_g, 16)
                gp.wait_ge(s_g, self.val(s_g))
                self.inc(gp.dma_scatter_add(tbl.ap(), buf[:, :, :], SLW[:, qq * 256:(qq + 1) * 256],
                                            4096, 4096, 64), s_sc, 16)
                scat_v[qq] = self.val(s_sc)
            RS["sc_scat"] = self.val(s_sc)

        te.wait_ge(s_v, v_oh_final)
        t_av = {}
        a_exp = {}
        v_pk = {}
        v_msk = {}
        t_sc = {}
        self._first_exp = True
        for hp in range(8):
            if hp == 1:
                emit_R1b()
            elif hp == 3:
                emit_R2()
            elif hp == 5:
                emit_R3()
            for qb in range(4):
                q0 = qb * 512
                ktmax = 4 * qb + 4
                bank_first = [True, True]
                obk = [PS[4 + 2 * (qb % 2)], PS[5 + 2 * (qb % 2)]]

                def emit_scores(kt, hp=hp, qb=qb, q0=q0):
                    dbuf = kt % 2
                    sd = SD[dbuf]
                    qoff = max(q0, kt * 128)
                    w = q0 + 512 - qoff
                    te.wait_ge(s_qkt, qkt_t[4 * qb + 3])
                    te.wait_ge(s_qf, qft_t[4 * qb + 3])
                    if (hp, qb, kt - 2) in a_exp:
                        te.wait_ge(s_a, a_exp[(hp, qb, kt - 2)])
                    elif kt < 2 and (hp, qb) != (0, 0):
                        te.wait_ge(s_a, self._prev(a_exp, hp, qb, kt % 2))
                    diag = (kt * 128 >= q0) and (kt * 128 < q0 + 512)
                    te.matmul(sd[:, 0, 0:w], KF[0:64, hp, kt * 128:(kt + 1) * 128],
                              QF[0:64, hp, qoff:q0 + 512], start=True, stop=True)
                    mmb = te.matmul(sd[:, 1, 0:w], KF[64:128, hp, kt * 128:(kt + 1) * 128],
                                    QF[64:128, hp, qoff:q0 + 512], start=True, stop=True,
                                    tile_position=(64, 0))
                    self.inc(mmb, s_t, 1)
                    t_sc[(hp, qb, kt)] = self.val(s_t)
                    if diag:
                        ve.wait_ge(s_t, t_sc[(hp, qb, kt)])
                        ve.tensor_tensor(sd[:, 0, 0:128], sd[:, 0, 0:128], TRI[:, :], ALU.add)
                        self.inc(ve.tensor_tensor(sd[:, 1, 0:128], sd[:, 1, 0:128], TRI[:, :], ALU.add), s_v, 1)
                        v_msk[(hp, qb, kt)] = self.val(s_v)
                    ac.wait_ge(s_t, t_sc[(hp, qb, kt)])
                    if diag:
                        ac.wait_ge(s_v, v_msk[(hp, qb, kt)])
                    if self._first_exp:
                        ac.wait_ge(s_sc, sc_zero)
                        ac.wait_ge(s_v, v_z)
                        self._first_exp = False
                    if (hp, qb, kt - 2) in t_av:
                        ac.wait_ge(s_t, t_av[(hp, qb, kt - 2)])
                    elif kt < 2 and (hp, qb) != (0, 0):
                        ac.wait_ge(s_t, self._prev(t_av, hp, qb, kt % 2))
                    EBv = EB[:, dbuf, :].rearrange("p (u f) -> p u f", u=2)
                    self.inc(ac.activation(EBv[:, :, 0:w], sd[:, :, 0:w], AF.Exp,
                                           bias=NEGSH[:, 0:1], scale=0.125), s_a, 1)
                    a_exp[(hp, qb, kt)] = self.val(s_a)

                def emit_av(kt, hp=hp, qb=qb, q0=q0, bank_first=bank_first, obk=obk):
                    dbuf = kt % 2
                    qoff = max(q0, kt * 128)
                    te.wait_ge(s_a, a_exp[(hp, qb, kt)])
                    if kt == 0 and (hp, qb) != (0, 0):
                        te.wait_ge(s_v, self._prevpk_par(v_pk, hp, qb))
                    lastmm = None
                    for j in range(4):
                        qc = 4 * qb + j
                        if qc < kt:
                            continue
                        erel = qc * 128 - qoff
                        for h01 in range(2):
                            lastmm = te.matmul(obk[h01][:, j * 65:(j + 1) * 65],
                                               EB[:, dbuf, h01 * 512 + erel:h01 * 512 + erel + 128],
                                               VB[:, kt, (2 * hp + h01) * 65:(2 * hp + h01 + 1) * 65],
                                               start=bank_first[h01], stop=False, skip_group_check=True)
                            bank_first[h01] = False
                    self.inc(lastmm, s_t, 1)
                    t_av[(hp, qb, kt)] = self.val(s_t)

                emit_scores(0)
                for kt in range(ktmax):
                    if kt + 1 < ktmax:
                        emit_scores(kt + 1)
                    emit_av(kt)
                ve.wait_ge(s_t, t_av[(hp, qb, ktmax - 1)])
                for jj in range(2):
                    jp = jj % 2
                    prev_ot = self.val(s_ot) - 16
                    if prev_ot > 0:
                        ve.wait_ge(s_ot, prev_ot)
                    pkv = PKD[:, jp, :].rearrange("p (u f) -> p u f", u=2)
                    lpk = None
                    for j2 in range(2):
                        j = 2 * jj + j2
                        ve.reciprocal(RLX[:, 0:1], obk[0][:, j * 65 + 64:j * 65 + 65])
                        ve.reciprocal(RLX[:, 1:2], obk[1][:, j * 65 + 64:j * 65 + 65])
                        ve.tensor_scalar(pkv[:, j2, 0:64], obk[0][:, j * 65:j * 65 + 64], RLX[:, 0:1], None, ALU.mult)
                        lpk = ve.tensor_scalar(pkv[:, j2, 64:128], obk[1][:, j * 65:j * 65 + 64], RLX[:, 1:2], None, ALU.mult)
                    self.inc(lpk, s_v, 1)
                    sync.wait_ge(s_v, self.val(s_v))
                    if self.val(s_ot) == 0:
                        sync.wait_ge(s_xn, 16 * NT)
                    base = (4 * qb + 2 * jj) * 128
                    self.inc(sync.dma_start(out=OF[:, hp, base:base + 256].rearrange("p (t f) -> p t f", t=2),
                                            in_=PKD[:, jp, :], transpose=True), s_ot, 16)
                v_pk[(hp, qb)] = self.val(s_v)
        ot_all = self.val(s_ot)
        t_attn_end = self.val(s_t)
        self.tap("of", OF[:, :, :].rearrange("p a b -> p (a b)"), (s_ot, ot_all))

        gp.wait_ge(s_v, RS["v_slot"])
        self.inc(gp.dma_start(out=WB[:, 0, :, :], in_=wo_d.ap().rearrange("(k p) o -> p k o", p=128)), s_w, 16)
        gp.wait_ge(s_v, vA)
        gp.wait_ge(s_sc, RS["sc_scat"])
        self.inc(gp.dma_start(out=WOBS[:, :], in_=wob_d.ap()), s_w, 16)
        w_wo = self.val(s_w)
        v_att_evac = 0
        a_att = 0
        v_pk_last = v_pk[(7, 3)]
        ve.wait_ge(s_sc, RS["sc_scat"])
        for tc in range(NT):
            for half in range(2):
                bank = PS[6 + half]
                te.wait_ge(s_ot, ot_all)
                te.wait_ge(s_v, v_pk_last)
                te.wait_ge(s_w, w_wo)
                if v_att_evac:
                    te.wait_ge(s_v, v_att_evac)
                for hp in range(8):
                    te.matmul(bank[:, :], OF[:, hp, tc * 128:(tc + 1) * 128],
                              WB[:, 0, hp, half * 512:(half + 1) * 512], start=(hp == 0), stop=False)
                self.inc(te.matmul(bank[:, :], ONB[0:1, 0:128], WOBS[0:1, half * 512:(half + 1) * 512],
                                   start=False, stop=True), s_t, 1)
                ve.wait_ge(s_t, self.val(s_t))
                if tc >= 2 and half == 0:
                    ve.wait_ge(s_att, 16 * (tc - 1))
                self.inc(ve.tensor_copy(ATS[:, tc % 2, half * 512:(half + 1) * 512], bank[:, :]), s_v, 1)
                v_att_evac = self.val(s_v)
            ac.wait_ge(s_v, v_att_evac)
            self.inc(ac.dma_start(out=att_d.ap()[tc * 128:(tc + 1) * 128, :], in_=ATS[:, tc % 2, :]), s_att, 16)
        att_all = self.val(s_att)
        t_oproj = self.val(s_t)
        self.tap("att", ATS[:, (NT - 1) % 2, :], (s_v, v_att_evac))

        gp.wait_ge(s_sc, RS["sc_scat"])
        gp.wait_ge(s_att, att_all)
        for el in range(EPC):
            base = el * (CAP + 1)
            self.inc(gp.dma_start(out=TRD[:, el, :],
                                  in_=tbl.ap()[base:base + 1024, 0:1].rearrange("(bb a) o -> a (bb o)", a=16),
                                  allow_slow_non_contiguous=True), s_g, 16)
        g_trd = self.val(s_g)
        ve.wait_ge(s_g, g_trd)
        trdf = TRD[:, :, :].rearrange("a e b -> a (e b)")
        ve.tensor_scalar(trdf, trdf, 1.0, None, ALU.subtract)
        ve.drain()
        ve.tensor_scalar(SF1[:, 0:128], trdf, 0.0, None, ALU.is_lt)
        ve.drain()
        ve.scalar_tensor_tensor(SF2[:, 0:128], SF1[:, 0:128], 16385.0, trdf, ALU.mult, ALU.add)
        ve.drain()
        self.inc(ve.tensor_copy(GIX[0:16, :, :].rearrange("a e b -> a (e b)"), SF2[:, 0:128]), s_v, 1)
        v_gix = self.val(s_v)
        sync.wait_ge(s_v, v_gix)
        for r in range(1, 8):
            self.inc(sync.dma_start(out=GIX[r * 16:(r + 1) * 16, :, :], in_=GIX[0:16, :, :]), s_d, 16)
        d_gix = self.val(s_d)

        self.tap("oht", OHT[:, :], (s_v, RS["v_excl"]))
        self.tap("excl", EXCL[:, :], (s_v, RS["v_excl"]))
        self.tap("loco", LOCO[:, :], (s_v, RS["v_loco"]))
        self.tap("lgo", LGO[:, :], (s_v, RS["v_padi"]))
        self.tap("gvo", GVO[:, :], (s_v, RS["v_padi"]))
        self.tap("cio", CIO[:, :], (s_v, RS["v_padi"]))
        self.tap("slw", SLW[0:16, :], (s_v, RS["v_slot"]))
        self.tap("gix", GIX[0:16, :, :].rearrange("a e b -> a (e b)"), (s_v, v_gix))
        self.tap("ciw", CIW[0:16, :], (s_g, RS["g_ciw"]))

        if _STOP_AFTER == "R":
            self._finish(y, VB, s_y)
            return


        if _STOP_AFTER == "B":
            self._finish(y, VB, s_y)
            return

        dfm = [QT, KT, QR, KR]
        for i in range(1, 4):
            assert self.nc.lookup_mloc(dfm[i]).addr - self.nc.lookup_mloc(QT).addr == 4096 * i
        DFM8 = self.alias("DFM8", QT, 0, [128, 8, 1024], BF16)

        def dfm_ap(k, lo, hi):
            return DFM8[:, k, lo:hi]

        def cap_ap(i):
            if i < 2:
                return SD0[:, i, :]
            if i < 4:
                return SD1[:, i - 2, :]
            return PS[i][:, :]

        dispv = VB[:, :, :].rearrange("p a b -> p (a b)")
        gp.wait_ge(s_cc, cc_xn)
        gp.wait_ge(s_d, d_gix)
        gp.wait_ge(s_t, t_attn_end)
        for el in range(EPC):
            self.inc(gp.dma_gather(dispv[:, el * 8192:(el + 1) * 8192].rearrange("p (cc x) -> p cc x", x=1024),
                                   xn_glob.ap(), GIX[:, el, :], CAP, CAP, 1024), s_g, 16)
        g_disp = self.val(s_g)

        t_htC = t_oproj
        t_ef_prev = None
        for el in range(EPC):
            if el == 1:
                gp.wait_ge(s_eo, self.val(s_eo))
                self.inc(gp.collective_compute("AllGather", ALU.bypass, replica_groups=[list(range(NCORES))],
                                               ins=[eo_self.ap()[0:CAP, :].opt()],
                                               outs=[eo_glob.ap()[0:NCORES * CAP, :].opt()]), s_cc, 1)
            sync.wait_ge(s_g, g_disp)
            sync.wait_ge(s_att, att_all)
            sync.wait_ge(s_v, v_att_evac)
            if el == 1:
                sync.wait_ge(s_t, t_htC)
            for cch in range(8):
                self.inc(sync.dma_start(out=DFM8[:, :, cch * 128:(cch + 1) * 128],
                                        in_=dispv[:, el * 8192 + cch * 1024:el * 8192 + (cch + 1) * 1024],
                                        transpose=True), s_ot, 16)
            ot_disp = self.val(s_ot)
            hps = [SD0[:, 0, :], SD0[:, 1, :], SD1[:, 0, :], SD1[:, 1, :]]
            a_expall = self.val(s_a)
            te.wait_ge(s_a, a_expall)
            if el == 1:
                te.wait_ge(s_v, self.val(s_v))
            t_hf = {}
            a_rl = {}
            w1v = {}

            def w1load(f):
                pbw = f % 2
                if f >= 2:
                    gp.wait_ge(s_t, t_hf[f - 2])
                self.inc(gp.dma_start(out=W1T[:, pbw, :, :],
                                      in_=w1_d[el].ap()[:, f * 128:(f + 1) * 128].rearrange("(k p) o -> p k o", p=128)),
                         s_w, 16)
                w1v[f] = self.val(s_w)

            w1load(0)
            w1load(1)
            for f in range(32):
                pbw = f % 2
                te.wait_ge(s_w, w1v[f])
                if f == 0:
                    te.wait_ge(s_ot, ot_disp)
                for half in range(2):
                    hb = hps[2 * pbw + half]
                    if f >= 2:
                        te.wait_ge(s_a, a_rl[(f - 2, half)])
                    mmh = None
                    for k in range(NKC):
                        mmh = te.matmul(hb, W1T[:, pbw, k, :], dfm_ap(k, half * 512, (half + 1) * 512),
                                        start=(k == 0), stop=(k == NKC - 1))
                    self.inc(mmh, s_t, 1)
                    if half == 1:
                        t_hf[f] = self.val(s_t)
                    ac.wait_ge(s_t, self.val(s_t))
                    HT = QF if f < 16 else KF
                    fo = f if f < 16 else f - 16
                    htv = HT[:, :, :].rearrange("p a b -> p (a b)")
                    self.inc(ac.activation(htv[:, fo * 1024 + half * 512:fo * 1024 + (half + 1) * 512],
                                           hb, AF.Relu), s_a, 1)
                    a_rl[(f, half)] = self.val(s_a)
                if f + 2 < 32:
                    w1load(f + 2)
            a_ht = self.val(s_a)
            t_htC = self.val(s_t)
            for dmh in range(2):
                v_eo_evac = self.val(s_v)
                t_ef = {}
                w2v = {}

                def w2load(f):
                    pbw = f % 2
                    if f >= 2:
                        gp.wait_ge(s_t, t_ef[f - 2])
                    elif t_ef_prev is not None:
                        gp.wait_ge(s_t, t_ef_prev[f + 30])
                    self.inc(gp.dma_start(out=W2T[:, pbw, 0:512],
                                          in_=w2_d[el].ap()[f * 128:(f + 1) * 128, dmh * 512:(dmh + 1) * 512]),
                             s_w, 16)
                    w2v[f] = self.val(s_w)

                w2load(0)
                w2load(1)
                for f in range(32):
                    pbw = f % 2
                    te.wait_ge(s_w, w2v[f])
                    te.wait_ge(s_a, a_ht)
                    if f == 0:
                        te.wait_ge(s_v, v_eo_evac)
                    tmm = None
                    for cap_c in range(8):
                        HTT = QF if f < 16 else KF
                        fo = f if f < 16 else f - 16
                        htv = HTT[:, :, :].rearrange("p a b -> p (a b)")
                        tmm = te.matmul(cap_ap(cap_c),
                                        htv[:, fo * 1024 + cap_c * 128:fo * 1024 + (cap_c + 1) * 128],
                                        W2T[:, pbw, 0:512],
                                        start=(f == 0), stop=(f == 31))
                    self.inc(tmm, s_t, 1)
                    t_ef[f] = self.val(s_t)
                    if f + 2 < 32:
                        w2load(f + 2)
                t_ef_prev = t_ef
                t_eo = self.val(s_t)
                ve.wait_ge(s_t, t_eo)
                for cap_c in range(8):
                    pbe = cap_c % 2
                    if self.val(s_eo) >= 32:
                        ve.wait_ge(s_eo, self.val(s_eo) - 16)
                    self.inc(ve.tensor_copy(EOS[:, pbe, 0:512], cap_ap(cap_c)), s_v, 1)
                    ac.wait_ge(s_v, self.val(s_v))
                    self.inc(ac.dma_start(out=eo_self.ap().rearrange("r d -> (r d)")
                                          [(el * CAP + cap_c * 128) * D + dmh * 512:]
                                          [0:128 * D - dmh * 512].rearrange("(p d) -> p d", d=D)[:, 0:512]
                                          if False else
                                          eo_self.ap()[el * CAP + cap_c * 128:el * CAP + (cap_c + 1) * 128,
                                                       dmh * 512:(dmh + 1) * 512],
                                          in_=EOS[:, pbe, 0:512]), s_eo, 16)
        eo_all = self.val(s_eo)
        t_expert_end = self.val(s_t)

        if _STOP_AFTER == "C":
            self._finish(y, VB, s_y)
            return

        gp.wait_ge(s_eo, eo_all)
        self.inc(gp.collective_compute("AllGather", ALU.bypass, replica_groups=[list(range(NCORES))],
                                       ins=[eo_self.ap()[CAP:2 * CAP, :].opt()],
                                       outs=[eo_glob.ap()[NCORES * CAP:2 * NCORES * CAP, :].opt()]), s_cc, 1)
        gp.wait_ge(s_cc, self.val(s_cc))
        gp.wait_ge(s_g, RS["g_ciw"])
        gp.wait_ge(s_t, t_expert_end)
        gfull = QF[:, :, :].rearrange("p a b -> p (a b)")
        if _STOP_AFTER == "D0":
            self._finish(y, VB, s_y)
            return
        g_half = []
        for gh in range(2):
            self.inc(gp.dma_gather(gfull[:, gh * 8192:(gh + 1) * 8192].rearrange("p (cc x) -> p cc x", x=1024),
                                   eo_glob.ap(), CIW[:, gh * 64:(gh + 1) * 64], 1024, 1024, 1024), s_g, 16)
            g_half.append(self.val(s_g))
        g_comb = self.val(s_g)
        if _STOP_AFTER == "D1":
            gp.wait_ge(s_g, g_comb)
            self._finish(y, VB, s_y)
            return

        sync.wait_ge(s_t, t_oproj)
        self.inc(sync.dma_start(out=XT[:, 0, :], in_=xin.ap()[0:128, :]), s_d, 16)
        sync.wait_ge(s_t, t_expert_end)
        self.inc(sync.dma_start(out=ATD[:, 0, :], in_=att_d.ap()[0:128, :]), s_d, 16)
        d_base = self.val(s_d)
        v_fin = 0
        for ct in range(NT):
            pb = ct % 2
            if ct + 1 < NT:
                if ct >= 1:
                    sync.wait_ge(s_v, v_fin)
                self.inc(sync.dma_start(out=XT[:, (ct + 1) % 2, :], in_=xin.ap()[(ct + 1) * 128:(ct + 2) * 128, :]), s_d, 16)
                self.inc(sync.dma_start(out=ATD[:, (ct + 1) % 2, :], in_=att_d.ap()[(ct + 1) * 128:(ct + 2) * 128, :]), s_d, 16)
            ve.wait_ge(s_g, g_half[ct // 8])
            ve.wait_ge(s_d, d_base + 32 * ct)
            if ct >= 2:
                ve.wait_ge([s_y, s_ya][ct % 2], 16 * (ct // 2))
            ve.tensor_scalar(XNT[:, pb, :], gfull[:, ct * 1024:(ct + 1) * 1024], GVO[:, ct:ct + 1], None, ALU.mult)
            ve.drain()
            ve.tensor_tensor(XNT[:, pb, :], XNT[:, pb, :], ATD[:, pb, :], ALU.add)
            ve.drain()
            self.inc(ve.tensor_tensor(XNT[:, pb, :], XNT[:, pb, :], XT[:, pb, :], ALU.add), s_v, 1)
            v_fin = self.val(s_v)
            qeng, qsem = [(ac, s_y), (sync, s_ya)][ct % 2]
            qeng.wait_ge(s_v, v_fin)
            self.inc(qeng.dma_start(out=y.ap()[ct * 128:(ct + 1) * 128, :], in_=XNT[:, pb, :]), qsem, 16)
        ac.wait_ge(s_y, self.val(s_y))
        ac.wait_ge(s_ya, self.val(s_ya))
        if self.semc[self.s_tap.name]:
            ac.wait_ge(self.s_tap, self.semc[self.s_tap.name])

    def _prev(self, dd, hp, qb, dbuf):
        best = 0
        for (h2, q2, k2), v in dd.items():
            if (h2, q2) < (hp, qb) and k2 % 2 == dbuf:
                best = max(best, v)
        return best

    def _prevpk(self, v_pk, hp, qb):
        best = 0
        for (h2, q2), v in v_pk.items():
            if (h2, q2) < (hp, qb):
                best = max(best, v)
        return best

    def _prevpk_par(self, v_pk, hp, qb):
        best = 0
        for (h2, q2), v in v_pk.items():
            if (h2, q2) < (hp, qb) and q2 % 2 == qb % 2:
                best = max(best, v)
        return best

    def _finish(self, y, VB, s_y):
        sync = self.nc.sync
        if self.semc[self.s_tap.name]:
            sync.wait_ge(self.s_tap, self.semc[self.s_tap.name])
        self.inc(self.nc.gpsimd.dma_start(out=y.ap()[0:128, :], in_=VB[:, 0, 0:1024]), s_y, 16)
        self.nc.gpsimd.wait_ge(s_y, self.val(s_y))


def _get_prog():
    key = (_STOP_AFTER, tuple(sorted(_TAPS)))
    if key not in _CACHE:
        k = _K()
        with k.ctx:
            k.build()
            k.nc.compile()
        _CACHE[key] = k
    return _CACHE[key]


def kernel(**inputs):
    in_maps = _hostprep(inputs)
    k = _get_prog()
    res = run_bass_kernel_spmd(k.nc, in_maps, core_ids=list(range(NCORES)))
    kernel._last = res
    out = np.stack([np.asarray(res.results[c]["y"]) for c in range(NCORES)], axis=0)
    return out.astype(np.float32)
